# version 03
# baseline (speedup 1.0000x reference)
import os
import numpy as np
import ml_dtypes
import concourse.bass as bass
import concourse.mybir as mybir
from concourse.bass_utils import run_bass_kernel_spmd

F32 = mybir.dt.float32
BF16 = mybir.dt.bfloat16
AF = mybir.ActivationFunctionType
ALU = mybir.AluOpType
AX = mybir.AxisListType
BFD = ml_dtypes.bfloat16
MAGIC = 12582912.0
ALPHA = float(1.0 / np.sqrt(np.float32(8192.0)))

T = 1024
D = 2048
H = 16
EPS_LN = 1e-5
EPS_RMS = 1e-6
EPS_FFN = 1e-5
NWINV = float(1.0 / (D * 4 * D))


def _hadamard(n):
    h = np.array([[1.0]], np.float32)
    while h.shape[0] < n:
        h = np.block([[h, h], [h, -h]]).astype(np.float32)
    return h


def _fwht_perm():
    q = np.arange(64)[:, None]
    p3 = np.arange(128)[None, :]
    return ((p3 % 64) * 128 + 2 * q + p3 // 64).reshape(-1)


def build():
    PH = int(os.environ.get("KPHASES", "99"))
    nc = bass.Bass("TRN2", target_bir_lowering=False)
    dr = lambda n, s, dt, k: nc.dram_tensor(n, s, dt, kind=k)
    xT = dr("xT", [D, T], F32, "ExternalInput")
    xprevT = dr("xprevT", [D, 2 * T], F32, "ExternalInput")
    Wqkvu = dr("Wqkvu", [D, 4 * D], F32, "ExternalInput")
    Wq4 = dr("Wq4", [64, 128, 2048], F32, "ExternalInput")
    Wo4 = dr("Wo4", [16, 128, 2048], F32, "ExternalInput")
    WgT = dr("WgT", [D, 4 * D], F32, "ExternalInput")
    WuT = dr("WuT", [D, 4 * D], F32, "ExternalInput")
    WdP2 = dr("WdP2", [16, 4 * D, 128], F32, "ExternalInput")
    sl_col = dr("sl_col", [128, H], F32, "ExternalInput")
    H128c = dr("H128c", [128, 128], BF16, "ExternalInput")
    BD64c = dr("BD64c", [128, 128], BF16, "ExternalInput")
    IDb = dr("IDb", [128, 128], BF16, "ExternalInput")
    ONESTc = dr("ONESTc", [128, 128], F32, "ExternalInput")
    SCONEc = dr("SCONEc", [128, 128], BF16, "ExternalInput")
    ONESBc = dr("ONESBc", [128, 128], BF16, "ExternalInput")
    ARR1 = dr("ARR1", [128, 256], F32, "ExternalInput")
    IDXD = dr("IDXD", [128, 512], F32, "ExternalInput")
    ARRC = dr("ARRC", [128, 8], F32, "ExternalInput")
    ARRB = dr("ARRB", [128, 2], F32, "ExternalInput")
    kqH = dr("kqH", [D, T], BF16, "Internal")
    oH = dr("oH", [D, T], BF16, "Internal")
    sH = dr("sH", [D, T], F32, "Internal")
    interH = dr("interH", [4 * D, T], F32, "Internal")
    outT = dr("outT", [D, T], F32, "ExternalOutput")

    sb = lambda n, s, dt: nc.alloc_sbuf_tensor(n, s, dt)
    ARA = sb("ARA", [128, 8192], F32)
    ARB = sb("ARB", [128, 32768], BF16)
    ARC = sb("ARC", [128, 16384], BF16)
    ARD = sb("ARD", [128, 16384], BF16)
    WS2 = sb("WS2", [128, 1024], BF16)
    WF0 = sb("WF0", [128, 512], F32)
    WF1 = sb("WF1", [128, 512], F32)
    EV0 = sb("EV0", [128, 512], F32)
    EV1 = sb("EV1", [128, 512], F32)
    EV2 = sb("EV2", [128, 512], F32)
    BCC = sb("BCC", [128, 1024], F32)
    BCD = sb("BCD", [128, 1024], F32)
    KV = sb("KV", [128, H * 128], F32)
    KVB = sb("KVB", [128, 128], BF16)
    QDB = sb("QDB", [128, 256], BF16)
    KDm = sb("KDm", [128, 256], BF16)
    KQR = sb("KQR", [128, 1024], BF16)
    FMX = sb("FMX", [128, 128], BF16)
    FMN = sb("FMN", [128, 128], BF16)
    HADt = sb("HADt", [128, 128], BF16)
    BDH = sb("BDH", [128, 128], BF16)
    IDT = sb("IDT", [128, 128], BF16)
    ONEST = sb("ONEST", [128, 128], F32)
    SCONE = sb("SCONE", [128, 128], BF16)
    ONESB = sb("ONESB", [128, 128], BF16)
    AR1 = sb("AR1", [128, 256], F32)
    IDX = sb("IDX", [128, 512], F32)
    ARC_ = sb("ARCt", [128, 8], F32)
    ARBt = sb("ARBt", [128, 2], F32)
    SLC = sb("SLC", [128, H], F32)
    SC0 = sb("SC0", [128, 1], F32)
    SC1 = sb("SC1", [128, 8], F32)
    SC2 = sb("SC2", [128, 8], F32)
    MGc = sb("MGc", [128, 6], F32)
    MGs = sb("MGs", [128, 3], F32)
    MGi = sb("MGi", [128, 3], F32)
    BP5 = sb("BP5", [128, 1], F32)
    BN5 = sb("BN5", [128, 1], F32)
    GSC = sb("GSC", [128, 3], F32)
    BDCOL = sb("BDCOL", [128, H], F32)
    W1COL = sb("W1COL", [128, H], F32)
    KDALL = sb("KDALL", [128, 32], F32)

    PSA = nc.alloc_psum_tensor("PSA", [128, 1024], F32)
    PSB = nc.alloc_psum_tensor("PSB", [128, 1024], F32)
    PSC = nc.alloc_psum_tensor("PSC", [128, 512], F32)
    PSD = nc.alloc_psum_tensor("PSD", [128, 512], F32)
    PSE = nc.alloc_psum_tensor("PSE", [128, 512], F32)
    PST = nc.alloc_psum_tensor("PST", [128, 1024], BF16)

    WS0 = ARA[:, 0:1024].bitcast(BF16)
    WS1 = ARA[:, 1024:2048].bitcast(BF16)
    WS3 = ARA[:, 7680:8192].bitcast(BF16)
    BCA = BCC[:, 0:512]
    BCB = BCD[:, 0:512]

    class Ctr:
        def __init__(self, name):
            self.s = nc.alloc_semaphore(name)
            self.n = 0
    LD, GW, TE, VE, AC, STS, BAR, GW0, GW1 = (Ctr(x) for x in
                                    ("LD", "GW", "TE", "VE", "AC", "STS", "BAR",
                                     "GW0", "GW1"))
    GWP = (GW0, GW1)
    OST = tuple(Ctr(f"OST{i}") for i in range(4))
    STG_S = tuple(Ctr(f"STG{i}") for i in range(3))
    _DRAIN_OPS = {"tensor_copy", "tensor_tensor", "tensor_scalar", "tensor_add",
                  "tensor_sub", "tensor_mul", "tensor_max", "scalar_tensor_tensor",
                  "tensor_reduce", "reduce_sum", "reduce_max", "reciprocal",
                  "memset", "activation", "tensor_single_scalar", "tensor_relu"}

    class _DW:
        def __init__(self, eng):
            self._e = eng

        def __getattr__(self, name):
            fn = getattr(self._e, name)
            if name in _DRAIN_OPS:
                def g(*a, **k):
                    self._e.drain()
                    return fn(*a, **k)
                return g
            return fn

    V, A = _DW(nc.vector), _DW(nc.scalar)
    TN, SY, GP = nc.tensor, nc.sync, nc.gpsimd
    ALLE = (nc.tensor, nc.vector, nc.scalar, SY, GP)

    def ld(dst, src):
        SY.dma_start(dst, src).then_inc(LD.s, 16)
        LD.n += 16

    def gld(dst, src):
        GP.dma_start(dst, src).then_inc(GW.s, 16)
        GW.n += 16

    def gldp(par, dst, src):
        s = GWP[par]
        GP.dma_start(dst, src).then_inc(s.s, 16)
        s.n += 16

    def vtick():
        V.drain().then_inc(VE.s, 1)
        VE.n += 1

    def atick():
        A.drain().then_inc(AC.s, 1)
        AC.n += 1

    def ttick():
        TN.drain().then_inc(TE.s, 1)
        TE.n += 1

    def barrier():
        for e in (nc.tensor, nc.vector, nc.scalar):
            e.drain().then_inc(BAR.s, 1)
        for e in (SY, GP):
            e.nop().then_inc(BAR.s, 1)
        BAR.n += len(ALLE)
        t = BAR.n
        for e in ALLE:
            e.wait_ge(BAR.s, t)

    def tail():
        A.dma_start(outT[0:128, 0:512], EV2[:, :]).then_inc(STS.s, 16)
        STS.n += 16
        A.wait_ge(STS.s, STS.n)

    def exp_tile(dst128, hcol, colap, sgn):
        V.tensor_scalar(dst128, ONEST[:, :], SLC[:, hcol:hcol + 1], colap, ALU.mult, ALU.mult)
        vtick()
        A.wait_ge(VE.s, VE.n)
        A.activation(dst128, dst128, AF.Exp, scale=sgn)
        atick()
        V.wait_ge(AC.s, AC.n)

    for dst, src in ((HADt, H128c), (BDH, BD64c), (IDT, IDb),
                     (ONEST, ONESTc), (SCONE, SCONEc), (ONESB, ONESBc),
                     (AR1, ARR1), (IDX, IDXD),
                     (ARC_, ARRC), (ARBt, ARRB), (SLC, sl_col)):
        ld(dst[:, :], src[:, :])
    V.wait_ge(LD.s, LD.n)
    V.memset(GSC[:, :], 0.0)
    V.tensor_scalar(BDCOL[:, :], SLC[:, :], -256.0, None, ALU.mult)
    V.tensor_scalar(W1COL[:, :], SLC[:, :], -1024.0, None, ALU.mult)
    for sh in range(2):
        V.tensor_scalar(KDALL[:, sh * 16:(sh + 1) * 16], SLC[:, :],
                        ARBt[:, sh:sh + 1], -1.0, ALU.mult, ALU.mult)
    vtick()
    A.wait_ge(VE.s, VE.n)
    A.activation(BDCOL[:, :], BDCOL[:, :], AF.Exp)
    A.activation(W1COL[:, :], W1COL[:, :], AF.Exp)
    A.activation(KDALL[:, :], KDALL[:, :], AF.Exp)
    atick()
    V.wait_ge(AC.s, AC.n)
    vtick()
    barrier()
    if PH <= 1:
        tail(); return nc

    def ln(xdram, col0, dst, dstcol0, dststride):
        for j in range(16):
            ld(ARA[:, j * 512:(j + 1) * 512], xdram[j * 128:(j + 1) * 128, col0:col0 + 512])
        TN.wait_ge(LD.s, LD.n)
        V.wait_ge(LD.s, LD.n)
        A.wait_ge(LD.s, LD.n)
        for j in range(16):
            if j > 0:
                V.wait_ge(TE.s, TE.n)
            V.tensor_copy(WS2[:, 0:512], ARA[:, j * 512:(j + 1) * 512])
            V.tensor_tensor(EV0[:, :], ARA[:, j * 512:(j + 1) * 512], WS2[:, 0:512], ALU.subtract)
            V.tensor_copy(WS2[:, 512:1024], EV0[:, :])
            vtick()
            TN.wait_ge(VE.s, VE.n)
            TN.matmul(PSC[:, :], SCONE[:, :], WS2[:, 0:512], start=(j == 0), stop=False)
            TN.matmul(PSC[:, :], SCONE[:, :], WS2[:, 512:1024], start=False,
                      stop=(j == 15))
            ttick()
            A.wait_ge(TE.s, TE.n)
            A.activation(WS2[:, 0:512], ARA[:, j * 512:(j + 1) * 512], AF.Square)
            atick()
            TN.wait_ge(AC.s, AC.n)
            TN.matmul(PSD[:, :], SCONE[:, :], WS2[:, 0:512], start=(j == 0), stop=(j == 15))
            ttick()
        V.wait_ge(TE.s, TE.n)
        V.tensor_copy(BCA[:, :], PSC[:, :])
        V.scalar_tensor_tensor(EV0[:, :], BCA[:, :], 1.0, BCA[:, :], ALU.bypass, ALU.mult)
        V.tensor_copy(EV1[:, :], PSD[:, :])
        V.tensor_tensor(EV1[:, :], EV1[:, :], EV0[:, :], ALU.subtract)
        V.tensor_scalar(EV1[:, :], EV1[:, :], EPS_LN, None, ALU.add)
        vtick()
        TN.wait_ge(VE.s, VE.n)
        A.wait_ge(VE.s, VE.n)
        A.activation(EV0[:, :], EV1[:, :], AF.Sqrt)
        atick()
        V.wait_ge(AC.s, AC.n)
        V.reciprocal(EV0[:, :], EV0[:, :])
        V.scalar_tensor_tensor(EV2[:, :], EV0[:, :], 1.0, EV0[:, :], ALU.bypass, ALU.mult)
        V.tensor_tensor(EV2[:, :], EV2[:, :], EV1[:, :], ALU.mult)
        V.tensor_scalar(EV2[:, :], EV2[:, :], -0.5, 1.5, ALU.mult, ALU.add)
        V.tensor_tensor(BCB[:, :], EV0[:, :], EV2[:, :], ALU.mult)
        for j in range(16):
            V.tensor_tensor(ARA[:, j * 512:(j + 1) * 512], ARA[:, j * 512:(j + 1) * 512],
                            BCA[:, :], ALU.subtract)
            V.tensor_tensor(dst[:, j * dststride + dstcol0:j * dststride + dstcol0 + 512],
                            ARA[:, j * 512:(j + 1) * 512], BCB[:, :], ALU.mult)
        vtick()
        SY.wait_ge(VE.s, VE.n)

    def bcast_col(dst128, colap):
        V.tensor_scalar(EV0[:, 0:128], ONEST[:, :], colap, None, ALU.mult)
        V.tensor_copy(WS2[:, 0:128], EV0[:, 0:128])
        V.tensor_tensor(EV1[:, 0:128], EV0[:, 0:128], WS2[:, 0:128], ALU.subtract)
        V.tensor_copy(WS2[:, 128:256], EV1[:, 0:128])
        vtick()
        TN.wait_ge(VE.s, VE.n)
        TN.matmul(PSE[:, 0:128], WS2[:, 0:128], IDT[:, :], start=True, stop=False)
        TN.matmul(PSE[:, 0:128], WS2[:, 128:256], IDT[:, :], start=False, stop=True)
        ttick()
        V.wait_ge(TE.s, TE.n)
        V.tensor_copy(dst128, PSE[:, 0:128])
        vtick()
        TN.wait_ge(VE.s, VE.n)

    for pc in range(2):
        ln(xprevT, pc * 1024, ARB, 0, 1024)
        if PH == 11:
            barrier(); tail(); return nc
        ln(xprevT, pc * 1024 + 512, ARB, 512, 1024)
        barrier()
        if PH == 12:
            tail(); return nc
        CGp = ARD[:, 0:8192]
        CUp = ARD[:, 8192:16384]
        for obp in range(4):
            kcol = 2048 + obp * 512
            vcol = 4096 + obp * 512
            GP.wait_ge(TE.s, TE.n)
            for hh in range(16):
                gld(CGp[:, hh * 512:(hh + 1) * 512],
                    Wqkvu[hh * 128:(hh + 1) * 128, kcol:kcol + 512])
                gld(CUp[:, hh * 512:(hh + 1) * 512],
                    Wqkvu[hh * 128:(hh + 1) * 128, vcol:vcol + 512])
            TN.wait_ge(GW.s, GW.n)
            for tt in range(8):
                for hl in range(4):
                    V.tensor_scalar(ARA[:, tt * 512 + hl * 128:tt * 512 + hl * 128 + 128],
                                    ONEST[:, :], SLC[:, obp * 4 + hl:obp * 4 + hl + 1],
                                    ARC_[:, tt:tt + 1], ALU.mult, ALU.mult)
            vtick()
            A.wait_ge(VE.s, VE.n)
            A.activation(ARA[:, 0:4096], ARA[:, 0:4096], AF.Exp, scale=-1.0)
            atick()
            V.wait_ge(AC.s, AC.n)
            PAIRS = ((PSA[:, 0:512], PSA[:, 512:1024]),
                     (PSB[:, 0:512], PSB[:, 512:1024]),
                     (PSC[:, :], PSD[:, :]))
            pair_ve = [0, 0, 0]
            pair_ac = [0, 0, 0]
            sacc_ve = 0
            SACC = BCC[:, 0:512]
            for tt in range(8):
                pk, pv = PAIRS[tt % 3]
                w2o = (tt % 2) * 512
                TN.wait_ge(VE.s, pair_ve[tt % 3])
                TN.wait_ge(AC.s, pair_ac[tt % 3])
                for hh in range(16):
                    TN.matmul(pk, ARB[:, hh * 1024 + tt * 128:hh * 1024 + tt * 128 + 128],
                              CGp[:, hh * 512:(hh + 1) * 512], start=(hh == 0), stop=(hh == 15))
                    TN.matmul(pv, ARB[:, hh * 1024 + tt * 128:hh * 1024 + tt * 128 + 128],
                              CUp[:, hh * 512:(hh + 1) * 512], start=(hh == 0), stop=(hh == 15))
                ttick()
                A.wait_ge(TE.s, TE.n)
                A.wait_ge(VE.s, sacc_ve)
                A.activation(WS2[:, w2o:w2o + 512], pk, AF.Silu)
                atick()
                pair_ac[tt % 3] = AC.n
                V.wait_ge(TE.s, TE.n)
                V.tensor_copy(WS3[:, w2o:w2o + 512], pv)
                V.wait_ge(AC.s, AC.n)
                V.tensor_tensor(WS2[:, w2o:w2o + 512], WS2[:, w2o:w2o + 512],
                                ARA[:, tt * 512:(tt + 1) * 512], ALU.mult)
                vtick()
                pair_ve[tt % 3] = VE.n
                TN.wait_ge(VE.s, VE.n)
                for hl in range(4):
                    TN.matmul(PSE[:, hl * 128:(hl + 1) * 128],
                              WS2[:, w2o + hl * 128:w2o + hl * 128 + 128],
                              WS3[:, w2o + hl * 128:w2o + hl * 128 + 128],
                              start=True, stop=True)
                ttick()
                V.wait_ge(TE.s, TE.n)
                if tt == 0:
                    V.tensor_copy(SACC[:, :], PSE[:, :])
                else:
                    V.tensor_tensor(SACC[:, :], SACC[:, :], PSE[:, :], ALU.add)
                vtick()
                sacc_ve = VE.n
                TN.wait_ge(VE.s, sacc_ve)
            V.wait_ge(TE.s, TE.n)
            if pc == 0:
                V.tensor_copy(KV[:, obp * 512:(obp + 1) * 512], SACC[:, :])
                vtick()
            else:
                for hl in range(4):
                    h = obp * 4 + hl
                    V.tensor_scalar(KV[:, h * 128:(h + 1) * 128], KV[:, h * 128:(h + 1) * 128],
                                    W1COL[:, h:h + 1], None, ALU.mult)
                    V.tensor_tensor(KV[:, h * 128:(h + 1) * 128], KV[:, h * 128:(h + 1) * 128],
                                    SACC[:, hl * 128:(hl + 1) * 128], ALU.add)
                vtick()
            TN.wait_ge(VE.s, VE.n)
            if PH == 13:
                barrier(); tail(); return nc
        barrier()
        if PH <= 2 + pc:
            tail(); return nc
    barrier()

    ln(xT, 0, ARC, 0, 1024)
    ln(xT, 512, ARC, 512, 1024)
    barrier()
    XN = ARC
    V.tensor_tensor(BCD[:, :], XN[:, 0:1024], XN[:, 1024:2048], ALU.max)
    V.tensor_tensor(BCC[:, 0:1024], XN[:, 0:1024], XN[:, 1024:2048], ALU.min)
    for j in range(2, 16):
        V.tensor_tensor(BCD[:, :], BCD[:, :], XN[:, j * 1024:(j + 1) * 1024], ALU.max)
        V.tensor_tensor(BCC[:, 0:1024], BCC[:, 0:1024], XN[:, j * 1024:(j + 1) * 1024], ALU.min)
    V.tensor_scalar(BCC[:, 0:1024], BCC[:, 0:1024], -1.0, None, ALU.mult)
    V.tensor_tensor(BCD[:, :], BCD[:, :], BCC[:, 0:1024], ALU.max)
    V.tensor_copy(WS2[:, 0:1024], BCD[:, :])
    vtick()
    TN.wait_ge(VE.s, VE.n)
    for tt in range(8):
        TN.transpose(PST[:, 0:128], WS2[:, tt * 128:(tt + 1) * 128], IDT[:, :])
        ttick()
        V.wait_ge(TE.s, TE.n)
        V.reduce_max(SC1[:, tt:tt + 1], PST[:, 0:128], AX.X)
        vtick()
        TN.wait_ge(VE.s, VE.n)
    V.tensor_scalar(SC1[:, 0:8], SC1[:, 0:8], EPS_FFN, None, ALU.max)
    V.reciprocal(SC2[:, 0:8], SC1[:, 0:8])
    V.tensor_scalar(SC2[:, 0:8], SC2[:, 0:8], 128.0, None, ALU.mult)
    vtick()
    for tt in range(8):
        bcast_col(BCC[:, tt * 128:(tt + 1) * 128], SC2[:, tt:tt + 1])
    for j in range(16):
        buf = WS2 if j % 2 == 0 else WS3
        if j >= 2:
            V.wait_ge(LD.s, LD.n)
        V.tensor_tensor(EV0[:, :], XN[:, j * 1024:j * 1024 + 512], BCC[:, 0:512], ALU.mult)
        V.tensor_tensor(EV1[:, :], XN[:, j * 1024 + 512:(j + 1) * 1024], BCC[:, 512:1024], ALU.mult)
        V.tensor_scalar(EV0[:, :], EV0[:, :], MAGIC, MAGIC, ALU.add, ALU.subtract)
        V.tensor_scalar(EV1[:, :], EV1[:, :], MAGIC, MAGIC, ALU.add, ALU.subtract)
        V.tensor_scalar(buf[:, 0:512], EV0[:, :], 127.0, None, ALU.min)
        V.tensor_scalar(buf[:, 512:1024], EV1[:, :], 127.0, None, ALU.min)
        vtick()
        SY.wait_ge(VE.s, VE.n)
        ld(kqH[j * 128:(j + 1) * 128, :], buf[:, 0:1024])
    V.wait_ge(LD.s, LD.n)
    barrier()
    if PH <= 4:
        tail(); return nc

    scr = lambda a, b: ARA[:, a:b]
    for HF in range(2):
        th0 = HF * 512
        for oc in range(32):
            buf = WS0 if oc % 2 == 0 else WS1
            if oc >= 2:
                GP.wait_ge(TE.s, TE.n)
            gldp(oc % 2, buf[:, :], Wq4[oc, :, :])
            TN.wait_ge(GWP[oc % 2].s, GWP[oc % 2].n)
            ps = PSA[:, 0:512] if oc % 2 == 0 else PSA[:, 512:1024]
            if oc >= 2:
                TN.wait_ge(AC.s, AC.n)
            for hh in range(16):
                TN.matmul(ps, buf[:, hh * 128:(hh + 1) * 128],
                          XN[:, hh * 1024 + th0:hh * 1024 + th0 + 512],
                          start=(hh == 0), stop=(hh == 15))
            ttick()
            A.wait_ge(TE.s, TE.n)
            if oc < 16:
                dst = ARB[:, oc * 512:(oc + 1) * 512]
            else:
                dst = ARB[:, 8192 + (oc - 16) * 512:8192 + (oc - 15) * 512]
            A.activation(dst, ps, AF.Silu)
            atick()
        for oc in range(48, 64):
            i = oc - 48
            buf = WS0 if i % 2 == 0 else WS1
            GP.wait_ge(TE.s, TE.n)
            gldp(i % 2, buf[:, :], Wq4[oc, :, :])
            TN.wait_ge(GWP[i % 2].s, GWP[i % 2].n)
            ps = PSB[:, 0:512] if i % 2 == 0 else PSB[:, 512:1024]
            if i >= 2:
                TN.wait_ge(VE.s, VE.n)
            for hh in range(16):
                TN.matmul(ps, buf[:, hh * 128:(hh + 1) * 128],
                          XN[:, hh * 1024 + th0:hh * 1024 + th0 + 512],
                          start=(hh == 0), stop=(hh == 15))
            ttick()
            V.wait_ge(TE.s, TE.n)
            V.tensor_copy(ARD[:, i * 1024 + th0:i * 1024 + th0 + 512], ps)
            vtick()
        A.wait_ge(TE.s, TE.n)
        V.wait_ge(TE.s, TE.n)
        TN.wait_ge(AC.s, AC.n)
        for ob in range(4):
            for hh in range(16):
                buf = WS0 if hh % 2 == 0 else WS1
                GP.wait_ge(TE.s, TE.n)
                gldp(hh % 2, buf[:, 0:512],
                     Wqkvu[hh * 128:(hh + 1) * 128, 4096 + ob * 512:4096 + ob * 512 + 512])
                TN.wait_ge(GWP[hh % 2].s, GWP[hh % 2].n)
                if hh == 0:
                    TN.wait_ge(VE.s, VE.n)
                for tl in range(4):
                    bank = (PSA[:, 0:512], PSA[:, 512:1024], PSB[:, 0:512], PSB[:, 512:1024])[tl]
                    TN.matmul(bank, XN[:, hh * 1024 + th0 + tl * 128:hh * 1024 + th0 + tl * 128 + 128],
                              buf[:, 0:512], start=(hh == 0), stop=(hh == 15))
                ttick()
            V.wait_ge(TE.s, TE.n)
            for tl in range(4):
                bank = (PSA[:, 0:512], PSA[:, 512:1024], PSB[:, 0:512], PSB[:, 512:1024])[tl]
                V.tensor_copy(ARB[:, 16384 + tl * 2048 + ob * 512:16384 + tl * 2048 + ob * 512 + 512], bank)
            vtick()
        TN.wait_ge(VE.s, VE.n)
        V.wait_ge(TE.s, TE.n)
        QDV = (BCC[:, :].bitcast(BF16), BCD[:, :].bitcast(BF16))
        for h in range(16):
            V.tensor_scalar(ARA[:, h * 256:(h + 1) * 256], AR1[:, :],
                            SLC[:, h:h + 1], None, ALU.mult)
        vtick()
        A.wait_ge(VE.s, VE.n)
        A.activation(ARA[:, 0:4096], ARA[:, 0:4096], AF.Exp, scale=-1.0)
        atick()
        V.wait_ge(AC.s, AC.n)
        V.tensor_copy(QDV[0][:, :], ARA[:, 0:2048])
        V.tensor_copy(QDV[1][:, :], ARA[:, 2048:4096])
        vtick()
        for h in range(16):
            V.tensor_scalar(ARA[:, h * 512:h * 512 + 256], IDX[:, 0:256],
                            SLC[:, h:h + 1], None, ALU.mult)
            V.tensor_scalar(ARA[:, h * 512 + 256:h * 512 + 512], IDX[:, 256:512],
                            SLC[:, h:h + 1], None, ALU.mult)
        vtick()
        A.wait_ge(VE.s, VE.n)
        A.activation(ARA[:, 0:8192], ARA[:, 0:8192], AF.Exp, scale=-1.0)
        atick()
        V.wait_ge(AC.s, AC.n)
        vtick()
        for bl in (2 * HF, 2 * HF + 1):
            bloc = (bl % 2) * 256
            tlb0 = 2 * (bl % 2)
            for h in range(16):
                qdv = QDV[h // 8][:, (h % 8) * 256:(h % 8) * 256 + 256]
                V.wait_ge(TE.s, TE.n)
                V.tensor_copy(KVB[:, :], KV[:, h * 128:(h + 1) * 128])
                V.tensor_tensor(QDB[:, :], ARB[:, h * 512 + bloc:h * 512 + bloc + 256],
                                qdv, ALU.mult)
                vtick()
                TN.wait_ge(VE.s, VE.n)
                for sh in range(2):
                    TN.matmul(PSB[:, sh * 256:sh * 256 + 256],
                              ARB[:, 8192 + h * 512 + bloc + sh * 128:8192 + h * 512 + bloc + sh * 128 + 128],
                              ARB[:, h * 512 + bloc:h * 512 + bloc + 256], start=True, stop=True)
                ttick()
                TN.matmul(PSA[:, 0:256], KVB[:, :], QDB[:, :], start=True, stop=False)
                V.wait_ge(TE.s, TE.n)
                V.tensor_tensor(WS2[:, 0:256], PSB[:, 0:256],
                                ARA[:, h * 512:h * 512 + 256], ALU.mult)
                V.tensor_tensor(WS2[:, 256:512], PSB[:, 256:512],
                                ARA[:, h * 512 + 256:h * 512 + 512], ALU.mult)
                vtick()
                TN.wait_ge(VE.s, VE.n)
                for sh in range(2):
                    TN.matmul(PSA[:, 0:256],
                              ARB[:, 16384 + (tlb0 + sh) * 2048 + h * 128:16384 + (tlb0 + sh) * 2048 + h * 128 + 128],
                              WS2[:, sh * 256:sh * 256 + 256], start=False, stop=(sh == 1))
                for sh in range(2):
                    TN.transpose(PST[:, sh * 128:sh * 128 + 128],
                                 ARB[:, 8192 + h * 512 + bloc + sh * 128:8192 + h * 512 + bloc + sh * 128 + 128],
                                 IDT[:, :])
                ttick()
                slot = h % 4
                V.wait_ge(TE.s, TE.n)
                V.wait_ge(OST[slot].s, OST[slot].n)
                V.tensor_copy(KQR[:, slot * 256:slot * 256 + 256], PSA[:, 0:256])
                V.tensor_scalar(KDm[:, 0:128], PST[:, 0:128],
                                KDALL[:, 0 * 16 + h:0 * 16 + h + 1], None, ALU.mult)
                V.tensor_scalar(KDm[:, 128:256], PST[:, 128:256],
                                KDALL[:, 1 * 16 + h:1 * 16 + h + 1], None, ALU.mult)
                vtick()
                SY.wait_ge(VE.s, VE.n)
                SY.dma_start(oH[h * 128:(h + 1) * 128, bl * 256:bl * 256 + 256],
                             KQR[:, slot * 256:slot * 256 + 256]).then_inc(OST[slot].s, 16)
                OST[slot].n += 16
                TN.wait_ge(VE.s, VE.n)
                for sh in range(2):
                    TN.matmul(PSC[:, 0:128], KDm[:, sh * 128:sh * 128 + 128],
                              ARB[:, 16384 + (tlb0 + sh) * 2048 + h * 128:16384 + (tlb0 + sh) * 2048 + h * 128 + 128],
                              start=(sh == 0), stop=(sh == 1))
                ttick()
                V.wait_ge(TE.s, TE.n)
                V.tensor_scalar(KV[:, h * 128:(h + 1) * 128], KV[:, h * 128:(h + 1) * 128],
                                BDCOL[:, h:h + 1], None, ALU.mult)
                V.tensor_tensor(KV[:, h * 128:(h + 1) * 128], KV[:, h * 128:(h + 1) * 128],
                                PSC[:, 0:128], ALU.add)
                vtick()
        barrier()
        if PH <= 5 + HF:
            tail(); return nc

    for s in OST:
        SY.wait_ge(s.s, s.n)
    for j in range(16):
        ld(ARB[:, j * 1024:(j + 1) * 1024], oH[j * 128:(j + 1) * 128, :])
    OTv = ARB
    for j in range(16):
        A.wait_ge(LD.s, LD.n)
        A.wait_ge(TE.s, TE.n)
        A.activation(WS2[:, 0:1024], OTv[:, j * 1024:(j + 1) * 1024], AF.Square)
        atick()
        TN.wait_ge(AC.s, AC.n)
        TN.matmul(PSC[:, :], ONESB[:, :], WS2[:, 0:512], start=(j == 0), stop=(j == 15))
        TN.matmul(PSD[:, :], ONESB[:, :], WS2[:, 512:1024], start=(j == 0), stop=(j == 15))
        ttick()
        A.wait_ge(TE.s, TE.n)
    V.wait_ge(TE.s, TE.n)
    for hf in range(2):
        ps = PSC if hf == 0 else PSD
        V.tensor_scalar(EV1[:, :], ps[:, :], 1.0 / 2048.0, EPS_RMS, ALU.mult, ALU.add)
        vtick()
        A.wait_ge(VE.s, VE.n)
        A.activation(EV0[:, :], EV1[:, :], AF.Sqrt)
        atick()
        V.wait_ge(AC.s, AC.n)
        V.reciprocal(EV0[:, :], EV0[:, :])
        V.scalar_tensor_tensor(EV2[:, :], EV0[:, :], 1.0, EV0[:, :], ALU.bypass, ALU.mult)
        V.tensor_tensor(EV2[:, :], EV2[:, :], EV1[:, :], ALU.mult)
        V.tensor_scalar(EV2[:, :], EV2[:, :], -0.5, 1.5, ALU.mult, ALU.add)
        V.tensor_tensor(BCC[:, hf * 512:hf * 512 + 512], EV0[:, :], EV2[:, :], ALU.mult)
        vtick()
    TN.wait_ge(VE.s, VE.n)
    for j in range(16):
        V.tensor_tensor(WS2[:, 0:1024], ARD[:, j * 1024:(j + 1) * 1024], BCC[:, :], ALU.mult)
        V.tensor_tensor(ARD[:, j * 1024:(j + 1) * 1024], WS2[:, 0:1024],
                        OTv[:, j * 1024:(j + 1) * 1024], ALU.mult)
        vtick()
    barrier()
    for oc in range(16):
        buf = WS0 if oc % 2 == 0 else WS1
        if oc >= 2:
            GP.wait_ge(TE.s, TE.n)
        gldp(oc % 2, buf[:, :], Wo4[oc, :, :])
        TN.wait_ge(GWP[oc % 2].s, GWP[oc % 2].n)
        for th in range(2):
            ps = PSA[:, 0:512] if th == 0 else PSA[:, 512:1024]
            if oc >= 1 or th >= 1:
                TN.wait_ge(VE.s, VE.n)
            for hh in range(16):
                TN.matmul(ps, buf[:, hh * 128:(hh + 1) * 128],
                          ARD[:, hh * 1024 + th * 512:hh * 1024 + th * 512 + 512],
                          start=(hh == 0), stop=(hh == 15))
            ttick()
            V.wait_ge(TE.s, TE.n)
            V.tensor_copy(EV2[:, :], ps)
            vtick()
            SY.wait_ge(VE.s, VE.n)
            ld(sH[oc * 128:(oc + 1) * 128, th * 512:th * 512 + 512], EV2[:, :])
            V.wait_ge(LD.s, LD.n)
    barrier()
    if PH <= 7:
        tail(); return nc

    MB = (ARA[:, 0:4096].bitcast(BF16), ARA[:, 4096:8192].bitcast(BF16))
    for mi, wsrc in enumerate((WgT, WuT, WdP2)):
        for ch in range(16):
            buf = MB[ch % 2]
            if mi * 16 + ch >= 2:
                GP.wait_ge(VE.s, VE.n)
            if mi < 2:
                gldp(ch % 2, buf[:, :], wsrc[ch * 128:(ch + 1) * 128, :])
            else:
                gldp(ch % 2, buf.rearrange("p (a o) -> p a o", a=64),
                     wsrc[ch, :, :].rearrange("(a p) o -> p a o", p=128))
            V.wait_ge(GWP[ch % 2].s, GWP[ch % 2].n)
            if ch == 0:
                V.reduce_sum(GSC[:, mi:mi + 1], buf[:, :], AX.X, apply_absolute_value=True)
            else:
                V.reduce_sum(SC0[:, :], buf[:, :], AX.X, apply_absolute_value=True)
                V.tensor_tensor(GSC[:, mi:mi + 1], GSC[:, mi:mi + 1], SC0[:, :], ALU.add)
            vtick()
    barrier()
    V.tensor_copy(WS2[:, 0:3], GSC[:, 0:3])
    V.tensor_tensor(EV0[:, 0:3], GSC[:, 0:3], WS2[:, 0:3], ALU.subtract)
    V.tensor_copy(WS2[:, 8:11], EV0[:, 0:3])
    vtick()
    TN.wait_ge(VE.s, VE.n)
    TN.matmul(PSE[:, 0:3], ONESB[:, :], WS2[:, 0:3], start=True, stop=False)
    TN.matmul(PSE[:, 0:3], ONESB[:, :], WS2[:, 8:11], start=False, stop=True)
    ttick()
    V.wait_ge(TE.s, TE.n)
    V.tensor_scalar(MGs[:, 0:3], PSE[:, 0:3], NWINV, EPS_FFN, ALU.mult, ALU.max)
    V.tensor_scalar(MGc[:, 0:3], MGs[:, 0:3], 0.5, None, ALU.mult)
    V.tensor_scalar(MGc[:, 3:6], MGs[:, 0:3], -0.5, None, ALU.mult)
    V.reciprocal(MGi[:, 0:3], MGs[:, 0:3])
    V.memset(BP5[:, :], 0.5)
    V.memset(BN5[:, :], -0.5)
    vtick()
    TN.wait_ge(VE.s, VE.n)
    barrier()
    if PH <= 8:
        tail(); return nc

    for j in range(16):
        ld(ARC[:, j * 1024:(j + 1) * 1024], kqH[j * 128:(j + 1) * 128, :])
    KQ = ARC
    CGt = ARD[:, 0:8192]
    CUt = ARD[:, 8192:16384]
    for tt in range(8):
        bcast_col(BCC[:, tt * 128:(tt + 1) * 128], SC1[:, tt:tt + 1])
    V.tensor_copy(BCD[:, :], BCC[:, :])
    V.tensor_scalar(BCC[:, :], BCC[:, :], MGs[:, 0:1], 1.0 / 256.0, ALU.mult, ALU.mult)
    V.tensor_scalar(BCD[:, :], BCD[:, :], MGs[:, 1:2], -1.0 / 128.0, ALU.mult, ALU.mult)
    vtick()
    V.wait_ge(LD.s, LD.n)
    vtick()
    STG = (EV2, WF0, WF1)
    stg_i = 0
    for og in range(16):
        for mat, wsrcT, cdst in ((0, WgT, CGt), (1, WuT, CUt)):
            SY.wait_ge(VE.s, VE.n)
            SY.wait_ge(AC.s, AC.n)
            ld(ARA[:, :].rearrange("p (a c) -> p a c", a=16),
               wsrcT[:, og * 512:og * 512 + 512].rearrange("(a p) c -> p a c", p=128))
            if mat == 0:
                A.wait_ge(LD.s, LD.n)
                for hh in range(16):
                    buf = WS2 if hh % 2 == 0 else KQR
                    if hh >= 2:
                        A.wait_ge(VE.s, VE.n)
                    A.activation(buf[:, 0:512], ARA[:, hh * 512:(hh + 1) * 512],
                                 AF.Sign, bias=BN5[:, :], scale=MGi[:, 0:1])
                    A.activation(buf[:, 512:1024], ARA[:, hh * 512:(hh + 1) * 512],
                                 AF.Sign, bias=BP5[:, :], scale=MGi[:, 0:1])
                    atick()
                    V.wait_ge(AC.s, AC.n)
                    V.tensor_tensor(cdst[:, hh * 512:(hh + 1) * 512],
                                    buf[:, 0:512], buf[:, 512:1024], ALU.add)
                    vtick()
            else:
                V.wait_ge(LD.s, LD.n)
                for hh in range(16):
                    V.tensor_scalar(WS2[:, 0:512], ARA[:, hh * 512:(hh + 1) * 512],
                                    MGc[:, 1:2], None, ALU.is_gt)
                    V.scalar_tensor_tensor(cdst[:, hh * 512:(hh + 1) * 512],
                                           ARA[:, hh * 512:(hh + 1) * 512],
                                           MGc[:, 4:5], WS2[:, 0:512],
                                           ALU.is_lt, ALU.subtract)
                vtick()
        TN.wait_ge(VE.s, VE.n)
        for sub in range(4):
            ocol = og * 512 + sub * 128
            if og > 0 or sub > 0:
                TN.wait_ge(VE.s, VE.n)
            for hh in range(16):
                TN.matmul(PSA[:, 0:512], CGt[:, hh * 512 + sub * 128:hh * 512 + sub * 128 + 128],
                          KQ[:, hh * 1024:hh * 1024 + 512], start=(hh == 0), stop=(hh == 15))
                TN.matmul(PSA[:, 512:1024], CGt[:, hh * 512 + sub * 128:hh * 512 + sub * 128 + 128],
                          KQ[:, hh * 1024 + 512:(hh + 1) * 1024], start=(hh == 0), stop=(hh == 15))
                TN.matmul(PSB[:, 0:512], CUt[:, hh * 512 + sub * 128:hh * 512 + sub * 128 + 128],
                          KQ[:, hh * 1024:hh * 1024 + 512], start=(hh == 0), stop=(hh == 15))
                TN.matmul(PSB[:, 512:1024], CUt[:, hh * 512 + sub * 128:hh * 512 + sub * 128 + 128],
                          KQ[:, hh * 1024 + 512:(hh + 1) * 1024], start=(hh == 0), stop=(hh == 15))
            ttick()
            for th in range(2):
                stg = STG[stg_i]
                V.wait_ge(TE.s, TE.n)
                V.wait_ge(STG_S[stg_i].s, STG_S[stg_i].n)
                V.tensor_tensor(EV0[:, :], PSB[:, th * 512:th * 512 + 512],
                                BCD[:, th * 512:th * 512 + 512], ALU.mult)
                V.tensor_tensor(EV1[:, :], PSA[:, th * 512:th * 512 + 512],
                                BCC[:, th * 512:th * 512 + 512], ALU.mult)
                vtick()
                A.wait_ge(VE.s, VE.n)
                A.activation(EV1[:, :], EV1[:, :], AF.Silu)
                atick()
                V.wait_ge(AC.s, AC.n)
                V.tensor_tensor(stg[:, 0:512], EV1[:, :], EV0[:, :], ALU.mult)
                vtick()
                GP.wait_ge(VE.s, VE.n)
                GP.dma_start(interH[ocol:ocol + 128, th * 512:th * 512 + 512],
                             stg[:, 0:512]).then_inc(STG_S[stg_i].s, 16)
                STG_S[stg_i].n += 16
                stg_i = (stg_i + 1) % 3
    barrier()
    if PH <= 9:
        tail(); return nc

    for s in STG_S:
        SY.wait_ge(s.s, s.n)
    FBv = ARC[:, 0:8192]
    SCR = ARC[:, 8192:9216]
    KD = ARB
    for th in range(2):
        for tf in range(4):
            t0 = th * 512 + tf * 128
            SY.wait_ge(VE.s, VE.n)
            for seg in range(8):
                ld(ARA[:, seg * 1024:(seg + 1) * 1024].rearrange("p (a c) -> p a c", a=8),
                   interH[seg * 1024:(seg + 1) * 1024, t0:t0 + 128]
                   .rearrange("(a p) c -> p a c", p=128))
            V.wait_ge(LD.s, LD.n)
            V.tensor_copy(CGt[:, 0:8192], ARA[:, 0:8192])
            V.memset(FMX[:, :], -1e30)
            V.memset(FMN[:, :], 1e30)
            vtick()
            TN.wait_ge(VE.s, VE.n)
            for i0 in range(0, 64, 4):
                ps = PSC if (i0 // 4) % 2 == 0 else PSD
                if i0 >= 8:
                    TN.wait_ge(VE.s, VE.n)
                for w in range(4):
                    TN.matmul(ps[:, w * 128:(w + 1) * 128],
                              CGt[:, (i0 + w) * 128:(i0 + w + 1) * 128],
                              HADt[:, :], start=True, stop=True)
                ttick()
                V.wait_ge(TE.s, TE.n)
                for w in range(4):
                    V.tensor_copy(FBv[:, i0 + w:i0 + w + 64 * 127 + 1:64],
                                  ps[:, w * 128:(w + 1) * 128])
                vtick()
            TN.wait_ge(VE.s, VE.n)
            for q0 in range(0, 64, 4):
                if q0 > 0:
                    TN.wait_ge(VE.s, VE.n)
                for w in range(4):
                    TN.transpose(PST[:, w * 128:(w + 1) * 128],
                                 FBv[:, (q0 + w) * 128:(q0 + w + 1) * 128], IDT[:, :])
                ttick()
                V.wait_ge(TE.s, TE.n)
                V.tensor_copy(SCR[:, 0:512], PST[:, 0:512])
                vtick()
                TN.wait_ge(VE.s, VE.n)
                for w in range(4):
                    TN.matmul(PSE[:, w * 128:(w + 1) * 128], BDH[:, :],
                              SCR[:, w * 128:(w + 1) * 128], start=True, stop=True)
                ttick()
                V.wait_ge(TE.s, TE.n)
                for w in range(4):
                    V.tensor_tensor(FMX[:, :], FMX[:, :], PSE[:, w * 128:(w + 1) * 128], ALU.max)
                    V.tensor_tensor(FMN[:, :], FMN[:, :], PSE[:, w * 128:(w + 1) * 128], ALU.min)
                V.tensor_copy(FBv[:, q0 * 128:(q0 + 4) * 128], PSE[:, 0:512])
                vtick()
            TN.wait_ge(VE.s, VE.n)
            V.tensor_scalar(FMN[:, :], FMN[:, :], -1.0, None, ALU.mult)
            V.tensor_tensor(FMX[:, :], FMX[:, :], FMN[:, :], ALU.max)
            vtick()
            TN.wait_ge(VE.s, VE.n)
            TN.transpose(PST[:, 0:128], FMX[:, :], IDT[:, :])
            ttick()
            V.wait_ge(TE.s, TE.n)
            V.reduce_max(SC1[:, 0:1], PST[:, 0:128], AX.X)
            V.tensor_scalar(SC1[:, 0:1], SC1[:, 0:1], ALPHA, EPS_FFN, ALU.mult, ALU.max)
            V.reciprocal(SC2[:, 0:1], SC1[:, 0:1])
            V.tensor_scalar(SC2[:, 0:1], SC2[:, 0:1], 128.0 * ALPHA, None, ALU.mult)
            V.tensor_scalar(SC1[:, 1:2], SC1[:, 0:1], MGs[:, 2:3], 1.0 / 256.0, ALU.mult, ALU.mult)
            vtick()
            TN.wait_ge(VE.s, VE.n)
            bcast_col(BCC[:, tf * 128:(tf + 1) * 128], SC2[:, 0:1])
            bcast_col(BCD[:, tf * 128:(tf + 1) * 128], SC1[:, 1:2])
            for q in range(64):
                V.tensor_tensor(EV0[:, 0:128], FBv[:, q * 128:(q + 1) * 128],
                                BCC[:, tf * 128:(tf + 1) * 128], ALU.mult)
                V.tensor_scalar(EV0[:, 0:128], EV0[:, 0:128], MAGIC, MAGIC, ALU.add, ALU.subtract)
                V.tensor_scalar(KD[:, q * 512 + tf * 128:q * 512 + tf * 128 + 128],
                                EV0[:, 0:128], 127.0, None, ALU.min)
            vtick()
        for o2 in range(16):
            SY.wait_ge(VE.s, VE.n)
            for seg in range(8):
                ld(ARA[:, seg * 1024:(seg + 1) * 1024].rearrange("p (a c) -> p a c", a=8),
                   WdP2[o2, seg * 1024:(seg + 1) * 1024, :].rearrange("(a p) c -> p a c", p=128))
            A.wait_ge(LD.s, LD.n)
            for gc in range(8):
                if gc >= 1:
                    A.wait_ge(VE.s, VE.n)
                A.activation(WS2[:, 0:1024], ARA[:, gc * 1024:(gc + 1) * 1024],
                             AF.Sign, bias=BN5[:, :], scale=MGi[:, 2:3])
                A.activation(KQR[:, 0:1024], ARA[:, gc * 1024:(gc + 1) * 1024],
                             AF.Sign, bias=BP5[:, :], scale=MGi[:, 2:3])
                atick()
                V.wait_ge(AC.s, AC.n)
                V.tensor_tensor(CUt[:, gc * 1024:(gc + 1) * 1024],
                                WS2[:, 0:1024], KQR[:, 0:1024], ALU.add)
                vtick()
            SY.wait_ge(VE.s, VE.n)
            ld(WF0[:, 0:512], xT[o2 * 128:(o2 + 1) * 128, th * 512:th * 512 + 512])
            ld(WF1[:, 0:512], sH[o2 * 128:(o2 + 1) * 128, th * 512:th * 512 + 512])
            TN.wait_ge(VE.s, VE.n)
            for g in range(64):
                TN.matmul(PSA[:, 0:512], CUt[:, g * 128:(g + 1) * 128],
                          KD[:, g * 512:g * 512 + 512], start=(g == 0), stop=(g == 63))
            ttick()
            V.wait_ge(TE.s, TE.n)
            V.tensor_tensor(EV0[:, :], PSA[:, 0:512], BCD[:, 0:512], ALU.mult)
            vtick()
            A.wait_ge(VE.s, VE.n)
            A.activation(EV1[:, :], EV0[:, :], AF.Silu)
            atick()
            V.wait_ge(AC.s, AC.n)
            V.wait_ge(LD.s, LD.n)
            V.tensor_tensor(EV2[:, :], EV1[:, :], WF1[:, 0:512], ALU.mult)
            V.tensor_tensor(EV2[:, :], EV2[:, :], WF0[:, 0:512], ALU.add)
            vtick()
            A.wait_ge(VE.s, VE.n)
            A.dma_start(outT[o2 * 128:(o2 + 1) * 128, th * 512:th * 512 + 512],
                        EV2[:, :]).then_inc(STS.s, 16)
            STS.n += 16
            A.wait_ge(STS.s, STS.n)
            TN.wait_ge(VE.s, VE.n)
    barrier()
    A.wait_ge(STS.s, STS.n)
    return nc


_NC_CACHE = {}


def kernel(L0_attn_x, L0_attn_slope_rate, L0_attn_W_qkvu, L0_attn_W_out,
           L0_ffn_w_gate, L0_ffn_w_up, L0_ffn_w_down):
    x = np.asarray(L0_attn_x, np.float32)[0]
    slope = np.asarray(L0_attn_slope_rate, np.float32)[:, 0, 0]
    WT = np.ascontiguousarray(np.asarray(L0_attn_W_qkvu, np.float32).T)
    WoT = np.ascontiguousarray(np.asarray(L0_attn_W_out, np.float32).T)
    WgTh = np.ascontiguousarray(np.asarray(L0_ffn_w_gate, np.float32).T)
    WuTh = np.ascontiguousarray(np.asarray(L0_ffn_w_up, np.float32).T)
    WdT = np.ascontiguousarray(np.asarray(L0_ffn_w_down, np.float32).T)
    WdTp = WdT[_fwht_perm(), :]
    WdP2 = np.ascontiguousarray(WdTp.reshape(8192, 16, 128).transpose(1, 0, 2))
    Wq4 = np.ascontiguousarray(WT.reshape(16, 128, 64, 128).transpose(2, 1, 0, 3)
                               .reshape(64, 128, 2048))
    Wo4 = np.ascontiguousarray(WoT.reshape(16, 128, 16, 128).transpose(2, 1, 0, 3)
                               .reshape(16, 128, 2048))
    H128 = _hadamard(128).astype(BFD)
    H64 = _hadamard(64)
    BD64 = np.zeros((128, 128), np.float32)
    BD64[:64, :64] = H64
    BD64[64:, 64:] = H64
    p = np.arange(128, dtype=np.float32)[:, None]
    consts = dict(
        H128c=H128, BD64c=BD64.astype(BFD),
        IDb=np.eye(128, dtype=np.float32).astype(BFD),
        ONESTc=np.ones((128, 128), np.float32),
        SCONEc=np.full((128, 128), 1.0 / 2048.0, np.float32).astype(BFD),
        ONESBc=np.ones((128, 128), np.float32).astype(BFD),
        ARR1=np.broadcast_to(np.arange(1, 257, dtype=np.float32)[None, :], (128, 256)).copy(),
        IDXD=np.concatenate([
            np.where(np.arange(256)[None, :] - (sh * 128 + p) >= 0,
                     np.arange(256, dtype=np.float32)[None, :] - (sh * 128 + p),
                     1e5).astype(np.float32)
            for sh in range(2)], axis=1),
        ARRC=np.concatenate([1023.0 - (tt * 128 + p) for tt in range(8)], axis=1).astype(np.float32),
        ARRB=np.concatenate([255.0 - (sh * 128 + p) for sh in range(2)], axis=1).astype(np.float32),
        sl_col=np.broadcast_to(slope[None, :], (128, 16)).copy(),
        Wqkvu=WT, Wq4=Wq4, Wo4=Wo4, WgT=WgTh, WuT=WuTh, WdP2=WdP2,
    )
    xall = np.ascontiguousarray(x.T)
    in_maps = []
    for c in range(8):
        xprev = np.zeros((2048, 2048), np.float32)
        if c >= 2:
            xprev[:, 0:1024] = xall[:, (c - 2) * 1024:(c - 1) * 1024]
        if c >= 1:
            xprev[:, 1024:2048] = xall[:, (c - 1) * 1024:c * 1024]
        m = dict(consts)
        m["xT"] = np.ascontiguousarray(xall[:, c * 1024:(c + 1) * 1024])
        m["xprevT"] = xprev
        in_maps.append(m)
    key = os.environ.get("KPHASES", "99")
    if key not in _NC_CACHE:
        _NC_CACHE[key] = build()
    res = run_bass_kernel_spmd(_NC_CACHE[key], in_maps, core_ids=list(range(8)))
    out = np.concatenate([res.results[c]["outT"].T for c in range(8)], axis=0)
    return np.ascontiguousarray(out, dtype=np.float32).reshape(1, 8192, 2048)


# version 06
# speedup vs baseline: 1.0086x; 1.0086x over previous
import os
import numpy as np
import ml_dtypes
import concourse.bass as bass
import concourse.mybir as mybir
from concourse.bass_utils import run_bass_kernel_spmd

F32 = mybir.dt.float32
BF16 = mybir.dt.bfloat16
AF = mybir.ActivationFunctionType
ALU = mybir.AluOpType
AX = mybir.AxisListType
BFD = ml_dtypes.bfloat16
MAGIC = 12582912.0
ALPHA = float(1.0 / np.sqrt(np.float32(8192.0)))

T = 1024
D = 2048
H = 16
EPS_LN = 1e-5
EPS_RMS = 1e-6
EPS_FFN = 1e-5
NWINV = float(1.0 / (D * 4 * D))


def _hadamard(n):
    h = np.array([[1.0]], np.float32)
    while h.shape[0] < n:
        h = np.block([[h, h], [h, -h]]).astype(np.float32)
    return h


def _fwht_perm():
    q = np.arange(64)[:, None]
    p3 = np.arange(128)[None, :]
    return ((p3 % 64) * 128 + 2 * q + p3 // 64).reshape(-1)


def build():
    PH = int(os.environ.get("KPHASES", "99"))
    nc = bass.Bass("TRN2", target_bir_lowering=False,
                   detect_race_conditions=False)
    dr = lambda n, s, dt, k: nc.dram_tensor(n, s, dt, kind=k)
    xT = dr("xT", [D, T], F32, "ExternalInput")
    xprevT = dr("xprevT", [D, 2 * T], F32, "ExternalInput")
    Wqkvu = dr("Wqkvu", [D, 4 * D], F32, "ExternalInput")
    Wq4 = dr("Wq4", [64, 128, 2048], F32, "ExternalInput")
    Wo4 = dr("Wo4", [16, 128, 2048], F32, "ExternalInput")
    WgT = dr("WgT", [D, 4 * D], F32, "ExternalInput")
    WuT = dr("WuT", [D, 4 * D], F32, "ExternalInput")
    WdP2 = dr("WdP2", [16, 4 * D, 128], F32, "ExternalInput")
    sl_col = dr("sl_col", [128, H], F32, "ExternalInput")
    H128c = dr("H128c", [128, 128], BF16, "ExternalInput")
    BD64c = dr("BD64c", [128, 128], BF16, "ExternalInput")
    IDb = dr("IDb", [128, 128], BF16, "ExternalInput")
    ONESTc = dr("ONESTc", [128, 128], F32, "ExternalInput")
    SCONEc = dr("SCONEc", [128, 128], BF16, "ExternalInput")
    ONESBc = dr("ONESBc", [128, 128], BF16, "ExternalInput")
    ARR1 = dr("ARR1", [128, 256], F32, "ExternalInput")
    IDXD = dr("IDXD", [128, 512], F32, "ExternalInput")
    ARRC = dr("ARRC", [128, 8], F32, "ExternalInput")
    ARRB = dr("ARRB", [128, 2], F32, "ExternalInput")
    kqH = dr("kqH", [D, T], BF16, "Internal")
    oH = dr("oH", [D, T], BF16, "Internal")
    sH = dr("sH", [D, T], F32, "Internal")
    interH = dr("interH", [4 * D, T], F32, "Internal")
    outT = dr("outT", [D, T], F32, "ExternalOutput")

    sb = lambda n, s, dt: nc.alloc_sbuf_tensor(n, s, dt)
    ARA = sb("ARA", [128, 8192], F32)
    ARB = sb("ARB", [128, 32768], BF16)
    ARC = sb("ARC", [128, 16384], BF16)
    ARD = sb("ARD", [128, 16384], BF16)
    WS2 = sb("WS2", [128, 1024], BF16)
    WF0 = sb("WF0", [128, 512], F32)
    WF1 = sb("WF1", [128, 512], F32)
    EV0 = sb("EV0", [128, 512], F32)
    EV1 = sb("EV1", [128, 512], F32)
    EV2 = sb("EV2", [128, 512], F32)
    BCC = sb("BCC", [128, 1024], F32)
    BCD = sb("BCD", [128, 1024], F32)
    KV = sb("KV", [128, H * 128], F32)
    KVB = sb("KVB", [128, 128], BF16)
    QDB = sb("QDB", [128, 256], BF16)
    KDm = sb("KDm", [128, 256], BF16)
    KQR = sb("KQR", [128, 1024], BF16)
    FMX = sb("FMX", [128, 128], BF16)
    FMN = sb("FMN", [128, 128], BF16)
    HADt = sb("HADt", [128, 128], BF16)
    BDH = sb("BDH", [128, 128], BF16)
    IDT = sb("IDT", [128, 128], BF16)
    ONEST = sb("ONEST", [128, 128], F32)
    SCONE = sb("SCONE", [128, 128], BF16)
    ONESB = sb("ONESB", [128, 128], BF16)
    AR1 = sb("AR1", [128, 256], F32)
    IDX = sb("IDX", [128, 512], F32)
    ARC_ = sb("ARCt", [128, 8], F32)
    ARBt = sb("ARBt", [128, 2], F32)
    SLC = sb("SLC", [128, H], F32)
    SC0 = sb("SC0", [128, 1], F32)
    SC1 = sb("SC1", [128, 8], F32)
    SC2 = sb("SC2", [128, 8], F32)
    MGc = sb("MGc", [128, 6], F32)
    MGs = sb("MGs", [128, 3], F32)
    MGi = sb("MGi", [128, 3], F32)
    BP5 = sb("BP5", [128, 1], F32)
    BN5 = sb("BN5", [128, 1], F32)
    GSC = sb("GSC", [128, 3], F32)
    BDCOL = sb("BDCOL", [128, H], F32)
    W1COL = sb("W1COL", [128, H], F32)
    KDALL = sb("KDALL", [128, 32], F32)

    PSA = nc.alloc_psum_tensor("PSA", [128, 1024], F32)
    PSB = nc.alloc_psum_tensor("PSB", [128, 1024], F32)
    PSC = nc.alloc_psum_tensor("PSC", [128, 512], F32)
    PSD = nc.alloc_psum_tensor("PSD", [128, 512], F32)
    PSE = nc.alloc_psum_tensor("PSE", [128, 512], F32)
    PST = nc.alloc_psum_tensor("PST", [128, 1024], BF16)

    WS0 = ARA[:, 0:1024].bitcast(BF16)
    WS1 = ARA[:, 1024:2048].bitcast(BF16)
    WS3 = ARA[:, 7680:8192].bitcast(BF16)
    BCA = BCC[:, 0:512]
    BCB = BCD[:, 0:512]

    class Ctr:
        def __init__(self, name):
            self.s = nc.alloc_semaphore(name)
            self.n = 0
    LD, GW, TE, VE, AC, STS, BAR, GW0, GW1 = (Ctr(x) for x in
                                    ("LD", "GW", "TE", "VE", "AC", "STS", "BAR",
                                     "GW0", "GW1"))
    GWP = (GW0, GW1)
    OST = tuple(Ctr(f"OST{i}") for i in range(4))
    STG_S = tuple(Ctr(f"STG{i}") for i in range(3))
    _DRAIN_OPS = {"tensor_copy", "tensor_tensor", "tensor_scalar", "tensor_add",
                  "tensor_sub", "tensor_mul", "tensor_max", "scalar_tensor_tensor",
                  "tensor_reduce", "reduce_sum", "reduce_max", "reciprocal",
                  "memset", "activation", "tensor_single_scalar", "tensor_relu"}

    class _DW:
        def __init__(self, eng):
            self._e = eng
            self.last = None
            self._dr = False

        def __getattr__(self, name):
            fn = getattr(self._e, name)
            if name in _DRAIN_OPS:
                def g(*a, **k):
                    if self._dr:
                        self._e.drain()
                    i = fn(*a, **k)
                    self.last = i
                    return i
                return g
            if name == "wait_ge":
                def w(*a, **k):
                    self.last = None
                    return fn(*a, **k)
                return w
            return fn

    DRM = os.environ.get("DRAINS", "va")
    V, A = _DW(nc.vector), _DW(nc.scalar)
    V._dr = "v" in DRM
    A._dr = "a" in DRM
    TN = _DW(nc.tensor)
    _TN_OPS = {"matmul", "transpose", "ldweights"}
    class _TNW:
        def __init__(self, eng):
            self._e = eng
            self.last = None
        def __getattr__(self, name):
            fn = getattr(self._e, name)
            if name in _TN_OPS:
                def g(*a, **k):
                    i = fn(*a, **k)
                    self.last = i
                    return i
                return g
            if name == "wait_ge":
                def w(*a, **k):
                    self.last = None
                    return fn(*a, **k)
                return w
            return fn
    TN = _TNW(nc.tensor)
    SY, GP = nc.sync, nc.gpsimd
    ALLE = (nc.tensor, nc.vector, nc.scalar, SY, GP)

    def ld(dst, src):
        SY.dma_start(dst, src).then_inc(LD.s, 16)
        LD.n += 16

    def gld(dst, src):
        GP.dma_start(dst, src).then_inc(GW.s, 16)
        GW.n += 16

    def gldp(par, dst, src):
        s = GWP[par]
        GP.dma_start(dst, src).then_inc(s.s, 16)
        s.n += 16

    def vtick():
        (V.last if V.last is not None else V._e.nop()).then_inc(VE.s, 1)
        V.last = None
        VE.n += 1

    def atick():
        (A.last if A.last is not None else A._e.nop()).then_inc(AC.s, 1)
        A.last = None
        AC.n += 1

    def ttick():
        (TN.last if TN.last is not None else TN._e.nop()).then_inc(TE.s, 1)
        TN.last = None
        TE.n += 1

    def barrier():
        for e in (nc.tensor, nc.vector, nc.scalar):
            e.drain().then_inc(BAR.s, 1)
        for e in (SY, GP):
            e.nop().then_inc(BAR.s, 1)
        BAR.n += len(ALLE)
        t = BAR.n
        for e in ALLE:
            e.wait_ge(BAR.s, t)

    def tail():
        A.dma_start(outT[0:128, 0:512], EV2[:, :]).then_inc(STS.s, 16)
        STS.n += 16
        A.wait_ge(STS.s, STS.n)

    def exp_tile(dst128, hcol, colap, sgn):
        V.tensor_scalar(dst128, ONEST[:, :], SLC[:, hcol:hcol + 1], colap, ALU.mult, ALU.mult)
        vtick()
        A.wait_ge(VE.s, VE.n)
        A.activation(dst128, dst128, AF.Exp, scale=sgn)
        atick()
        V.wait_ge(AC.s, AC.n)

    for dst, src in ((HADt, H128c), (BDH, BD64c), (IDT, IDb),
                     (ONEST, ONESTc), (SCONE, SCONEc), (ONESB, ONESBc),
                     (AR1, ARR1), (IDX, IDXD),
                     (ARC_, ARRC), (ARBt, ARRB), (SLC, sl_col)):
        ld(dst[:, :], src[:, :])
    V.wait_ge(LD.s, LD.n)
    V.memset(GSC[:, :], 0.0)
    V.tensor_scalar(BDCOL[:, :], SLC[:, :], -256.0, None, ALU.mult)
    V.tensor_scalar(W1COL[:, :], SLC[:, :], -1024.0, None, ALU.mult)
    for sh in range(2):
        V.tensor_scalar(KDALL[:, sh * 16:(sh + 1) * 16], SLC[:, :],
                        ARBt[:, sh:sh + 1], -1.0, ALU.mult, ALU.mult)
    vtick()
    A.wait_ge(VE.s, VE.n)
    A.activation(BDCOL[:, :], BDCOL[:, :], AF.Exp)
    A.activation(W1COL[:, :], W1COL[:, :], AF.Exp)
    A.activation(KDALL[:, :], KDALL[:, :], AF.Exp)
    atick()
    V.wait_ge(AC.s, AC.n)
    vtick()
    barrier()
    if PH <= 1:
        tail(); return nc

    def ln(xdram, col0, dst, dstcol0, dststride):
        for j in range(16):
            ld(ARA[:, j * 512:(j + 1) * 512], xdram[j * 128:(j + 1) * 128, col0:col0 + 512])
        TN.wait_ge(LD.s, LD.n)
        V.wait_ge(LD.s, LD.n)
        A.wait_ge(LD.s, LD.n)
        for j in range(16):
            if j > 0:
                V.wait_ge(TE.s, TE.n)
            V.tensor_copy(WS2[:, 0:512], ARA[:, j * 512:(j + 1) * 512])
            V.tensor_tensor(EV0[:, :], ARA[:, j * 512:(j + 1) * 512], WS2[:, 0:512], ALU.subtract)
            V.tensor_copy(WS2[:, 512:1024], EV0[:, :])
            vtick()
            TN.wait_ge(VE.s, VE.n)
            TN.matmul(PSC[:, :], SCONE[:, :], WS2[:, 0:512], start=(j == 0), stop=False)
            TN.matmul(PSC[:, :], SCONE[:, :], WS2[:, 512:1024], start=False,
                      stop=(j == 15))
            ttick()
            A.wait_ge(TE.s, TE.n)
            A.activation(WS2[:, 0:512], ARA[:, j * 512:(j + 1) * 512], AF.Square)
            atick()
            TN.wait_ge(AC.s, AC.n)
            TN.matmul(PSD[:, :], SCONE[:, :], WS2[:, 0:512], start=(j == 0), stop=(j == 15))
            ttick()
        V.wait_ge(TE.s, TE.n)
        V.tensor_copy(BCA[:, :], PSC[:, :])
        V.scalar_tensor_tensor(EV0[:, :], BCA[:, :], 1.0, BCA[:, :], ALU.bypass, ALU.mult)
        V.tensor_copy(EV1[:, :], PSD[:, :])
        V.tensor_tensor(EV1[:, :], EV1[:, :], EV0[:, :], ALU.subtract)
        V.tensor_scalar(EV1[:, :], EV1[:, :], EPS_LN, None, ALU.add)
        vtick()
        TN.wait_ge(VE.s, VE.n)
        A.wait_ge(VE.s, VE.n)
        A.activation(EV0[:, :], EV1[:, :], AF.Sqrt)
        atick()
        V.wait_ge(AC.s, AC.n)
        V.reciprocal(EV0[:, :], EV0[:, :])
        V.scalar_tensor_tensor(EV2[:, :], EV0[:, :], 1.0, EV0[:, :], ALU.bypass, ALU.mult)
        V.tensor_tensor(EV2[:, :], EV2[:, :], EV1[:, :], ALU.mult)
        V.tensor_scalar(EV2[:, :], EV2[:, :], -0.5, 1.5, ALU.mult, ALU.add)
        V.tensor_tensor(BCB[:, :], EV0[:, :], EV2[:, :], ALU.mult)
        for j in range(16):
            V.tensor_tensor(ARA[:, j * 512:(j + 1) * 512], ARA[:, j * 512:(j + 1) * 512],
                            BCA[:, :], ALU.subtract)
            V.tensor_tensor(dst[:, j * dststride + dstcol0:j * dststride + dstcol0 + 512],
                            ARA[:, j * 512:(j + 1) * 512], BCB[:, :], ALU.mult)
        vtick()
        SY.wait_ge(VE.s, VE.n)

    def bcast_col(dst128, colap):
        V.tensor_scalar(EV0[:, 0:128], ONEST[:, :], colap, None, ALU.mult)
        V.tensor_copy(WS2[:, 0:128], EV0[:, 0:128])
        V.tensor_tensor(EV1[:, 0:128], EV0[:, 0:128], WS2[:, 0:128], ALU.subtract)
        V.tensor_copy(WS2[:, 128:256], EV1[:, 0:128])
        vtick()
        TN.wait_ge(VE.s, VE.n)
        TN.matmul(PSE[:, 0:128], WS2[:, 0:128], IDT[:, :], start=True, stop=False)
        TN.matmul(PSE[:, 0:128], WS2[:, 128:256], IDT[:, :], start=False, stop=True)
        ttick()
        V.wait_ge(TE.s, TE.n)
        V.tensor_copy(dst128, PSE[:, 0:128])
        vtick()
        TN.wait_ge(VE.s, VE.n)

    for pc in range(2):
        ln(xprevT, pc * 1024, ARB, 0, 1024)
        if PH == 11:
            barrier(); tail(); return nc
        ln(xprevT, pc * 1024 + 512, ARB, 512, 1024)
        barrier()
        if PH == 12:
            tail(); return nc
        CGp = ARD[:, 0:8192]
        CUp = ARD[:, 8192:16384]
        for obp in range(4):
            kcol = 2048 + obp * 512
            vcol = 4096 + obp * 512
            GP.wait_ge(TE.s, TE.n)
            for hh in range(16):
                gld(CGp[:, hh * 512:(hh + 1) * 512],
                    Wqkvu[hh * 128:(hh + 1) * 128, kcol:kcol + 512])
                gld(CUp[:, hh * 512:(hh + 1) * 512],
                    Wqkvu[hh * 128:(hh + 1) * 128, vcol:vcol + 512])
            TN.wait_ge(GW.s, GW.n)
            for tt in range(8):
                for hl in range(4):
                    V.tensor_scalar(ARA[:, tt * 512 + hl * 128:tt * 512 + hl * 128 + 128],
                                    ONEST[:, :], SLC[:, obp * 4 + hl:obp * 4 + hl + 1],
                                    ARC_[:, tt:tt + 1], ALU.mult, ALU.mult)
            vtick()
            A.wait_ge(VE.s, VE.n)
            A.activation(ARA[:, 0:4096], ARA[:, 0:4096], AF.Exp, scale=-1.0)
            atick()
            V.wait_ge(AC.s, AC.n)
            PAIRS = ((PSA[:, 0:512], PSA[:, 512:1024]),
                     (PSB[:, 0:512], PSB[:, 512:1024]),
                     (PSC[:, :], PSD[:, :]))
            pair_ve = [0, 0, 0]
            pair_ac = [0, 0, 0]
            sacc_ve = 0
            SACC = BCC[:, 0:512]
            for tt in range(8):
                pk, pv = PAIRS[tt % 3]
                w2o = (tt % 2) * 512
                TN.wait_ge(VE.s, pair_ve[tt % 3])
                TN.wait_ge(AC.s, pair_ac[tt % 3])
                for hh in range(16):
                    TN.matmul(pk, ARB[:, hh * 1024 + tt * 128:hh * 1024 + tt * 128 + 128],
                              CGp[:, hh * 512:(hh + 1) * 512], start=(hh == 0), stop=(hh == 15))
                    TN.matmul(pv, ARB[:, hh * 1024 + tt * 128:hh * 1024 + tt * 128 + 128],
                              CUp[:, hh * 512:(hh + 1) * 512], start=(hh == 0), stop=(hh == 15))
                ttick()
                A.wait_ge(TE.s, TE.n)
                A.wait_ge(VE.s, sacc_ve)
                A.activation(WS2[:, w2o:w2o + 512], pk, AF.Silu)
                atick()
                pair_ac[tt % 3] = AC.n
                V.wait_ge(TE.s, TE.n)
                V.tensor_copy(WS3[:, w2o:w2o + 512], pv)
                V.wait_ge(AC.s, AC.n)
                V.tensor_tensor(WS2[:, w2o:w2o + 512], WS2[:, w2o:w2o + 512],
                                ARA[:, tt * 512:(tt + 1) * 512], ALU.mult)
                vtick()
                pair_ve[tt % 3] = VE.n
                TN.wait_ge(VE.s, VE.n)
                for hl in range(4):
                    TN.matmul(PSE[:, hl * 128:(hl + 1) * 128],
                              WS2[:, w2o + hl * 128:w2o + hl * 128 + 128],
                              WS3[:, w2o + hl * 128:w2o + hl * 128 + 128],
                              start=True, stop=True)
                ttick()
                V.wait_ge(TE.s, TE.n)
                if tt == 0:
                    V.tensor_copy(SACC[:, :], PSE[:, :])
                else:
                    V.tensor_tensor(SACC[:, :], SACC[:, :], PSE[:, :], ALU.add)
                vtick()
                sacc_ve = VE.n
                TN.wait_ge(VE.s, sacc_ve)
            V.wait_ge(TE.s, TE.n)
            if pc == 0:
                V.tensor_copy(KV[:, obp * 512:(obp + 1) * 512], SACC[:, :])
                vtick()
            else:
                for hl in range(4):
                    h = obp * 4 + hl
                    V.tensor_scalar(KV[:, h * 128:(h + 1) * 128], KV[:, h * 128:(h + 1) * 128],
                                    W1COL[:, h:h + 1], None, ALU.mult)
                    V.tensor_tensor(KV[:, h * 128:(h + 1) * 128], KV[:, h * 128:(h + 1) * 128],
                                    SACC[:, hl * 128:(hl + 1) * 128], ALU.add)
                vtick()
            TN.wait_ge(VE.s, VE.n)
            if PH == 13:
                barrier(); tail(); return nc
        barrier()
        if PH <= 2 + pc:
            tail(); return nc
    barrier()

    ln(xT, 0, ARC, 0, 1024)
    ln(xT, 512, ARC, 512, 1024)
    barrier()
    XN = ARC
    V.tensor_tensor(BCD[:, :], XN[:, 0:1024], XN[:, 1024:2048], ALU.max)
    V.tensor_tensor(BCC[:, 0:1024], XN[:, 0:1024], XN[:, 1024:2048], ALU.min)
    for j in range(2, 16):
        V.tensor_tensor(BCD[:, :], BCD[:, :], XN[:, j * 1024:(j + 1) * 1024], ALU.max)
        V.tensor_tensor(BCC[:, 0:1024], BCC[:, 0:1024], XN[:, j * 1024:(j + 1) * 1024], ALU.min)
    V.tensor_scalar(BCC[:, 0:1024], BCC[:, 0:1024], -1.0, None, ALU.mult)
    V.tensor_tensor(BCD[:, :], BCD[:, :], BCC[:, 0:1024], ALU.max)
    V.tensor_copy(WS2[:, 0:1024], BCD[:, :])
    vtick()
    TN.wait_ge(VE.s, VE.n)
    for tt in range(8):
        TN.transpose(PST[:, 0:128], WS2[:, tt * 128:(tt + 1) * 128], IDT[:, :])
        ttick()
        V.wait_ge(TE.s, TE.n)
        V.reduce_max(SC1[:, tt:tt + 1], PST[:, 0:128], AX.X)
        vtick()
        TN.wait_ge(VE.s, VE.n)
    V.tensor_scalar(SC1[:, 0:8], SC1[:, 0:8], EPS_FFN, None, ALU.max)
    V.reciprocal(SC2[:, 0:8], SC1[:, 0:8])
    V.tensor_scalar(SC2[:, 0:8], SC2[:, 0:8], 128.0, None, ALU.mult)
    vtick()
    for tt in range(8):
        bcast_col(BCC[:, tt * 128:(tt + 1) * 128], SC2[:, tt:tt + 1])
    for j in range(16):
        buf = WS2 if j % 2 == 0 else WS3
        if j >= 2:
            V.wait_ge(LD.s, LD.n)
        V.tensor_tensor(EV0[:, :], XN[:, j * 1024:j * 1024 + 512], BCC[:, 0:512], ALU.mult)
        V.tensor_tensor(EV1[:, :], XN[:, j * 1024 + 512:(j + 1) * 1024], BCC[:, 512:1024], ALU.mult)
        V.tensor_scalar(EV0[:, :], EV0[:, :], MAGIC, MAGIC, ALU.add, ALU.subtract)
        V.tensor_scalar(EV1[:, :], EV1[:, :], MAGIC, MAGIC, ALU.add, ALU.subtract)
        V.tensor_scalar(buf[:, 0:512], EV0[:, :], 127.0, None, ALU.min)
        V.tensor_scalar(buf[:, 512:1024], EV1[:, :], 127.0, None, ALU.min)
        vtick()
        SY.wait_ge(VE.s, VE.n)
        ld(kqH[j * 128:(j + 1) * 128, :], buf[:, 0:1024])
    V.wait_ge(LD.s, LD.n)
    barrier()
    if PH <= 4:
        tail(); return nc

    scr = lambda a, b: ARA[:, a:b]
    for HF in range(2):
        th0 = HF * 512
        for oc in range(32):
            buf = WS0 if oc % 2 == 0 else WS1
            if oc >= 2:
                GP.wait_ge(TE.s, TE.n)
            gldp(oc % 2, buf[:, :], Wq4[oc, :, :])
            TN.wait_ge(GWP[oc % 2].s, GWP[oc % 2].n)
            ps = PSA[:, 0:512] if oc % 2 == 0 else PSA[:, 512:1024]
            if oc >= 2:
                TN.wait_ge(AC.s, AC.n)
            for hh in range(16):
                TN.matmul(ps, buf[:, hh * 128:(hh + 1) * 128],
                          XN[:, hh * 1024 + th0:hh * 1024 + th0 + 512],
                          start=(hh == 0), stop=(hh == 15))
            ttick()
            A.wait_ge(TE.s, TE.n)
            if oc < 16:
                dst = ARB[:, oc * 512:(oc + 1) * 512]
            else:
                dst = ARB[:, 8192 + (oc - 16) * 512:8192 + (oc - 15) * 512]
            A.activation(dst, ps, AF.Silu)
            atick()
        for oc in range(48, 64):
            i = oc - 48
            buf = WS0 if i % 2 == 0 else WS1
            GP.wait_ge(TE.s, TE.n)
            gldp(i % 2, buf[:, :], Wq4[oc, :, :])
            TN.wait_ge(GWP[i % 2].s, GWP[i % 2].n)
            ps = PSB[:, 0:512] if i % 2 == 0 else PSB[:, 512:1024]
            if i >= 2:
                TN.wait_ge(VE.s, VE.n)
            for hh in range(16):
                TN.matmul(ps, buf[:, hh * 128:(hh + 1) * 128],
                          XN[:, hh * 1024 + th0:hh * 1024 + th0 + 512],
                          start=(hh == 0), stop=(hh == 15))
            ttick()
            V.wait_ge(TE.s, TE.n)
            V.tensor_copy(ARD[:, i * 1024 + th0:i * 1024 + th0 + 512], ps)
            vtick()
        A.wait_ge(TE.s, TE.n)
        V.wait_ge(TE.s, TE.n)
        TN.wait_ge(AC.s, AC.n)
        for ob in range(4):
            for hh in range(16):
                buf = WS0 if hh % 2 == 0 else WS1
                GP.wait_ge(TE.s, TE.n)
                gldp(hh % 2, buf[:, 0:512],
                     Wqkvu[hh * 128:(hh + 1) * 128, 4096 + ob * 512:4096 + ob * 512 + 512])
                TN.wait_ge(GWP[hh % 2].s, GWP[hh % 2].n)
                if hh == 0:
                    TN.wait_ge(VE.s, VE.n)
                for tl in range(4):
                    bank = (PSA[:, 0:512], PSA[:, 512:1024], PSB[:, 0:512], PSB[:, 512:1024])[tl]
                    TN.matmul(bank, XN[:, hh * 1024 + th0 + tl * 128:hh * 1024 + th0 + tl * 128 + 128],
                              buf[:, 0:512], start=(hh == 0), stop=(hh == 15))
                ttick()
            V.wait_ge(TE.s, TE.n)
            for tl in range(4):
                bank = (PSA[:, 0:512], PSA[:, 512:1024], PSB[:, 0:512], PSB[:, 512:1024])[tl]
                V.tensor_copy(ARB[:, 16384 + tl * 2048 + ob * 512:16384 + tl * 2048 + ob * 512 + 512], bank)
            vtick()
        TN.wait_ge(VE.s, VE.n)
        V.wait_ge(TE.s, TE.n)
        QDV = (BCC[:, :].bitcast(BF16), BCD[:, :].bitcast(BF16))
        for h in range(16):
            V.tensor_scalar(ARA[:, h * 256:(h + 1) * 256], AR1[:, :],
                            SLC[:, h:h + 1], None, ALU.mult)
        vtick()
        A.wait_ge(VE.s, VE.n)
        A.activation(ARA[:, 0:4096], ARA[:, 0:4096], AF.Exp, scale=-1.0)
        atick()
        V.wait_ge(AC.s, AC.n)
        V.tensor_copy(QDV[0][:, :], ARA[:, 0:2048])
        V.tensor_copy(QDV[1][:, :], ARA[:, 2048:4096])
        vtick()
        for h in range(16):
            V.tensor_scalar(ARA[:, h * 512:h * 512 + 256], IDX[:, 0:256],
                            SLC[:, h:h + 1], None, ALU.mult)
            V.tensor_scalar(ARA[:, h * 512 + 256:h * 512 + 512], IDX[:, 256:512],
                            SLC[:, h:h + 1], None, ALU.mult)
        vtick()
        A.wait_ge(VE.s, VE.n)
        A.activation(ARA[:, 0:8192], ARA[:, 0:8192], AF.Exp, scale=-1.0)
        atick()
        V.wait_ge(AC.s, AC.n)
        vtick()
        for bl in (2 * HF, 2 * HF + 1):
            bloc = (bl % 2) * 256
            tlb0 = 2 * (bl % 2)
            for h in range(16):
                qdv = QDV[h // 8][:, (h % 8) * 256:(h % 8) * 256 + 256]
                V.wait_ge(TE.s, TE.n)
                V.tensor_copy(KVB[:, :], KV[:, h * 128:(h + 1) * 128])
                V.tensor_tensor(QDB[:, :], ARB[:, h * 512 + bloc:h * 512 + bloc + 256],
                                qdv, ALU.mult)
                vtick()
                TN.wait_ge(VE.s, VE.n)
                for sh in range(2):
                    TN.matmul(PSB[:, sh * 256:sh * 256 + 256],
                              ARB[:, 8192 + h * 512 + bloc + sh * 128:8192 + h * 512 + bloc + sh * 128 + 128],
                              ARB[:, h * 512 + bloc:h * 512 + bloc + 256], start=True, stop=True)
                ttick()
                TN.matmul(PSA[:, 0:256], KVB[:, :], QDB[:, :], start=True, stop=False)
                V.wait_ge(TE.s, TE.n)
                V.tensor_tensor(WS2[:, 0:256], PSB[:, 0:256],
                                ARA[:, h * 512:h * 512 + 256], ALU.mult)
                V.tensor_tensor(WS2[:, 256:512], PSB[:, 256:512],
                                ARA[:, h * 512 + 256:h * 512 + 512], ALU.mult)
                vtick()
                TN.wait_ge(VE.s, VE.n)
                for sh in range(2):
                    TN.matmul(PSA[:, 0:256],
                              ARB[:, 16384 + (tlb0 + sh) * 2048 + h * 128:16384 + (tlb0 + sh) * 2048 + h * 128 + 128],
                              WS2[:, sh * 256:sh * 256 + 256], start=False, stop=(sh == 1))
                for sh in range(2):
                    TN.transpose(PST[:, sh * 128:sh * 128 + 128],
                                 ARB[:, 8192 + h * 512 + bloc + sh * 128:8192 + h * 512 + bloc + sh * 128 + 128],
                                 IDT[:, :])
                ttick()
                slot = h % 4
                V.wait_ge(TE.s, TE.n)
                V.wait_ge(OST[slot].s, OST[slot].n)
                V.tensor_copy(KQR[:, slot * 256:slot * 256 + 256], PSA[:, 0:256])
                V.tensor_scalar(KDm[:, 0:128], PST[:, 0:128],
                                KDALL[:, 0 * 16 + h:0 * 16 + h + 1], None, ALU.mult)
                V.tensor_scalar(KDm[:, 128:256], PST[:, 128:256],
                                KDALL[:, 1 * 16 + h:1 * 16 + h + 1], None, ALU.mult)
                vtick()
                SY.wait_ge(VE.s, VE.n)
                SY.dma_start(oH[h * 128:(h + 1) * 128, bl * 256:bl * 256 + 256],
                             KQR[:, slot * 256:slot * 256 + 256]).then_inc(OST[slot].s, 16)
                OST[slot].n += 16
                TN.wait_ge(VE.s, VE.n)
                for sh in range(2):
                    TN.matmul(PSC[:, 0:128], KDm[:, sh * 128:sh * 128 + 128],
                              ARB[:, 16384 + (tlb0 + sh) * 2048 + h * 128:16384 + (tlb0 + sh) * 2048 + h * 128 + 128],
                              start=(sh == 0), stop=(sh == 1))
                ttick()
                V.wait_ge(TE.s, TE.n)
                V.tensor_scalar(KV[:, h * 128:(h + 1) * 128], KV[:, h * 128:(h + 1) * 128],
                                BDCOL[:, h:h + 1], None, ALU.mult)
                V.tensor_tensor(KV[:, h * 128:(h + 1) * 128], KV[:, h * 128:(h + 1) * 128],
                                PSC[:, 0:128], ALU.add)
                vtick()
        barrier()
        if PH <= 5 + HF:
            tail(); return nc

    for s in OST:
        SY.wait_ge(s.s, s.n)
    for j in range(16):
        ld(ARB[:, j * 1024:(j + 1) * 1024], oH[j * 128:(j + 1) * 128, :])
    OTv = ARB
    for j in range(16):
        A.wait_ge(LD.s, LD.n)
        A.wait_ge(TE.s, TE.n)
        A.activation(WS2[:, 0:1024], OTv[:, j * 1024:(j + 1) * 1024], AF.Square)
        atick()
        TN.wait_ge(AC.s, AC.n)
        TN.matmul(PSC[:, :], ONESB[:, :], WS2[:, 0:512], start=(j == 0), stop=(j == 15))
        TN.matmul(PSD[:, :], ONESB[:, :], WS2[:, 512:1024], start=(j == 0), stop=(j == 15))
        ttick()
        A.wait_ge(TE.s, TE.n)
    V.wait_ge(TE.s, TE.n)
    for hf in range(2):
        ps = PSC if hf == 0 else PSD
        V.tensor_scalar(EV1[:, :], ps[:, :], 1.0 / 2048.0, EPS_RMS, ALU.mult, ALU.add)
        vtick()
        A.wait_ge(VE.s, VE.n)
        A.activation(EV0[:, :], EV1[:, :], AF.Sqrt)
        atick()
        V.wait_ge(AC.s, AC.n)
        V.reciprocal(EV0[:, :], EV0[:, :])
        V.scalar_tensor_tensor(EV2[:, :], EV0[:, :], 1.0, EV0[:, :], ALU.bypass, ALU.mult)
        V.tensor_tensor(EV2[:, :], EV2[:, :], EV1[:, :], ALU.mult)
        V.tensor_scalar(EV2[:, :], EV2[:, :], -0.5, 1.5, ALU.mult, ALU.add)
        V.tensor_tensor(BCC[:, hf * 512:hf * 512 + 512], EV0[:, :], EV2[:, :], ALU.mult)
        vtick()
    TN.wait_ge(VE.s, VE.n)
    for j in range(16):
        V.tensor_tensor(WS2[:, 0:1024], ARD[:, j * 1024:(j + 1) * 1024], BCC[:, :], ALU.mult)
        V.tensor_tensor(ARD[:, j * 1024:(j + 1) * 1024], WS2[:, 0:1024],
                        OTv[:, j * 1024:(j + 1) * 1024], ALU.mult)
        vtick()
    barrier()
    for oc in range(16):
        buf = WS0 if oc % 2 == 0 else WS1
        if oc >= 2:
            GP.wait_ge(TE.s, TE.n)
        gldp(oc % 2, buf[:, :], Wo4[oc, :, :])
        TN.wait_ge(GWP[oc % 2].s, GWP[oc % 2].n)
        for th in range(2):
            ps = PSA[:, 0:512] if th == 0 else PSA[:, 512:1024]
            if oc >= 1 or th >= 1:
                TN.wait_ge(VE.s, VE.n)
            for hh in range(16):
                TN.matmul(ps, buf[:, hh * 128:(hh + 1) * 128],
                          ARD[:, hh * 1024 + th * 512:hh * 1024 + th * 512 + 512],
                          start=(hh == 0), stop=(hh == 15))
            ttick()
            V.wait_ge(TE.s, TE.n)
            V.tensor_copy(EV2[:, :], ps)
            vtick()
            SY.wait_ge(VE.s, VE.n)
            ld(sH[oc * 128:(oc + 1) * 128, th * 512:th * 512 + 512], EV2[:, :])
            V.wait_ge(LD.s, LD.n)
    barrier()
    if PH <= 7:
        tail(); return nc

    MB = (ARA[:, 0:4096].bitcast(BF16), ARA[:, 4096:8192].bitcast(BF16))
    for mi, wsrc in enumerate((WgT, WuT, WdP2)):
        for ch in range(16):
            buf = MB[ch % 2]
            if mi * 16 + ch >= 2:
                GP.wait_ge(VE.s, VE.n)
            if mi < 2:
                gldp(ch % 2, buf[:, :], wsrc[ch * 128:(ch + 1) * 128, :])
            else:
                gldp(ch % 2, buf.rearrange("p (a o) -> p a o", a=64),
                     wsrc[ch, :, :].rearrange("(a p) o -> p a o", p=128))
            V.wait_ge(GWP[ch % 2].s, GWP[ch % 2].n)
            if ch == 0:
                V.reduce_sum(GSC[:, mi:mi + 1], buf[:, :], AX.X, apply_absolute_value=True)
            else:
                V.reduce_sum(SC0[:, :], buf[:, :], AX.X, apply_absolute_value=True)
                V.tensor_tensor(GSC[:, mi:mi + 1], GSC[:, mi:mi + 1], SC0[:, :], ALU.add)
            vtick()
    barrier()
    V.tensor_copy(WS2[:, 0:3], GSC[:, 0:3])
    V.tensor_tensor(EV0[:, 0:3], GSC[:, 0:3], WS2[:, 0:3], ALU.subtract)
    V.tensor_copy(WS2[:, 8:11], EV0[:, 0:3])
    vtick()
    TN.wait_ge(VE.s, VE.n)
    TN.matmul(PSE[:, 0:3], ONESB[:, :], WS2[:, 0:3], start=True, stop=False)
    TN.matmul(PSE[:, 0:3], ONESB[:, :], WS2[:, 8:11], start=False, stop=True)
    ttick()
    V.wait_ge(TE.s, TE.n)
    V.tensor_scalar(MGs[:, 0:3], PSE[:, 0:3], NWINV, EPS_FFN, ALU.mult, ALU.max)
    V.tensor_scalar(MGc[:, 0:3], MGs[:, 0:3], 0.5, None, ALU.mult)
    V.tensor_scalar(MGc[:, 3:6], MGs[:, 0:3], -0.5, None, ALU.mult)
    V.reciprocal(MGi[:, 0:3], MGs[:, 0:3])
    V.memset(BP5[:, :], 0.5)
    V.memset(BN5[:, :], -0.5)
    vtick()
    TN.wait_ge(VE.s, VE.n)
    barrier()
    if PH <= 8:
        tail(); return nc

    for j in range(16):
        ld(ARC[:, j * 1024:(j + 1) * 1024], kqH[j * 128:(j + 1) * 128, :])
    KQ = ARC
    CGt = ARD[:, 0:8192]
    CUt = ARD[:, 8192:16384]
    for tt in range(8):
        bcast_col(BCC[:, tt * 128:(tt + 1) * 128], SC1[:, tt:tt + 1])
    V.tensor_copy(BCD[:, :], BCC[:, :])
    V.tensor_scalar(BCC[:, :], BCC[:, :], MGs[:, 0:1], 1.0 / 256.0, ALU.mult, ALU.mult)
    V.tensor_scalar(BCD[:, :], BCD[:, :], MGs[:, 1:2], -1.0 / 128.0, ALU.mult, ALU.mult)
    vtick()
    V.wait_ge(LD.s, LD.n)
    vtick()
    STG = (EV2, WF0, WF1)
    stg_i = 0
    for og in range(16):
        for mat, wsrcT, cdst in ((0, WgT, CGt), (1, WuT, CUt)):
            SY.wait_ge(VE.s, VE.n)
            SY.wait_ge(AC.s, AC.n)
            ld(ARA[:, :].rearrange("p (a c) -> p a c", a=16),
               wsrcT[:, og * 512:og * 512 + 512].rearrange("(a p) c -> p a c", p=128))
            if mat == 0:
                A.wait_ge(LD.s, LD.n)
                for hh in range(16):
                    buf = WS2 if hh % 2 == 0 else KQR
                    if hh >= 2:
                        A.wait_ge(VE.s, VE.n)
                    A.activation(buf[:, 0:512], ARA[:, hh * 512:(hh + 1) * 512],
                                 AF.Sign, bias=BN5[:, :], scale=MGi[:, 0:1])
                    A.activation(buf[:, 512:1024], ARA[:, hh * 512:(hh + 1) * 512],
                                 AF.Sign, bias=BP5[:, :], scale=MGi[:, 0:1])
                    atick()
                    V.wait_ge(AC.s, AC.n)
                    V.tensor_tensor(cdst[:, hh * 512:(hh + 1) * 512],
                                    buf[:, 0:512], buf[:, 512:1024], ALU.add)
                    vtick()
            else:
                V.wait_ge(LD.s, LD.n)
                for hh in range(16):
                    V.tensor_scalar(WS2[:, 0:512], ARA[:, hh * 512:(hh + 1) * 512],
                                    MGc[:, 1:2], None, ALU.is_gt)
                    V.scalar_tensor_tensor(cdst[:, hh * 512:(hh + 1) * 512],
                                           ARA[:, hh * 512:(hh + 1) * 512],
                                           MGc[:, 4:5], WS2[:, 0:512],
                                           ALU.is_lt, ALU.subtract)
                vtick()
        TN.wait_ge(VE.s, VE.n)
        for sub in range(4):
            ocol = og * 512 + sub * 128
            if og > 0 or sub > 0:
                TN.wait_ge(VE.s, VE.n)
            for hh in range(16):
                TN.matmul(PSA[:, 0:512], CGt[:, hh * 512 + sub * 128:hh * 512 + sub * 128 + 128],
                          KQ[:, hh * 1024:hh * 1024 + 512], start=(hh == 0), stop=(hh == 15))
                TN.matmul(PSA[:, 512:1024], CGt[:, hh * 512 + sub * 128:hh * 512 + sub * 128 + 128],
                          KQ[:, hh * 1024 + 512:(hh + 1) * 1024], start=(hh == 0), stop=(hh == 15))
                TN.matmul(PSB[:, 0:512], CUt[:, hh * 512 + sub * 128:hh * 512 + sub * 128 + 128],
                          KQ[:, hh * 1024:hh * 1024 + 512], start=(hh == 0), stop=(hh == 15))
                TN.matmul(PSB[:, 512:1024], CUt[:, hh * 512 + sub * 128:hh * 512 + sub * 128 + 128],
                          KQ[:, hh * 1024 + 512:(hh + 1) * 1024], start=(hh == 0), stop=(hh == 15))
            ttick()
            for th in range(2):
                stg = STG[stg_i]
                V.wait_ge(TE.s, TE.n)
                V.wait_ge(STG_S[stg_i].s, STG_S[stg_i].n)
                V.tensor_tensor(EV0[:, :], PSB[:, th * 512:th * 512 + 512],
                                BCD[:, th * 512:th * 512 + 512], ALU.mult)
                V.tensor_tensor(EV1[:, :], PSA[:, th * 512:th * 512 + 512],
                                BCC[:, th * 512:th * 512 + 512], ALU.mult)
                vtick()
                A.wait_ge(VE.s, VE.n)
                A.activation(EV1[:, :], EV1[:, :], AF.Silu)
                atick()
                V.wait_ge(AC.s, AC.n)
                V.tensor_tensor(stg[:, 0:512], EV1[:, :], EV0[:, :], ALU.mult)
                vtick()
                GP.wait_ge(VE.s, VE.n)
                GP.dma_start(interH[ocol:ocol + 128, th * 512:th * 512 + 512],
                             stg[:, 0:512]).then_inc(STG_S[stg_i].s, 16)
                STG_S[stg_i].n += 16
                stg_i = (stg_i + 1) % 3
    barrier()
    if PH <= 9:
        tail(); return nc

    for s in STG_S:
        SY.wait_ge(s.s, s.n)
    FBv = ARC[:, 0:8192]
    SCR = ARC[:, 8192:9216]
    KD = ARB
    for th in range(2):
        for tf in range(4):
            t0 = th * 512 + tf * 128
            SY.wait_ge(VE.s, VE.n)
            for seg in range(8):
                ld(ARA[:, seg * 1024:(seg + 1) * 1024].rearrange("p (a c) -> p a c", a=8),
                   interH[seg * 1024:(seg + 1) * 1024, t0:t0 + 128]
                   .rearrange("(a p) c -> p a c", p=128))
            V.wait_ge(LD.s, LD.n)
            V.tensor_copy(CGt[:, 0:8192], ARA[:, 0:8192])
            V.memset(FMX[:, :], -1e30)
            V.memset(FMN[:, :], 1e30)
            vtick()
            TN.wait_ge(VE.s, VE.n)
            for i0 in range(0, 64, 4):
                ps = PSC if (i0 // 4) % 2 == 0 else PSD
                if i0 >= 8:
                    TN.wait_ge(VE.s, VE.n)
                for w in range(4):
                    TN.matmul(ps[:, w * 128:(w + 1) * 128],
                              CGt[:, (i0 + w) * 128:(i0 + w + 1) * 128],
                              HADt[:, :], start=True, stop=True)
                ttick()
                V.wait_ge(TE.s, TE.n)
                for w in range(4):
                    V.tensor_copy(FBv[:, i0 + w:i0 + w + 64 * 127 + 1:64],
                                  ps[:, w * 128:(w + 1) * 128])
                vtick()
            TN.wait_ge(VE.s, VE.n)
            for q0 in range(0, 64, 4):
                if q0 > 0:
                    TN.wait_ge(VE.s, VE.n)
                for w in range(4):
                    TN.transpose(PST[:, w * 128:(w + 1) * 128],
                                 FBv[:, (q0 + w) * 128:(q0 + w + 1) * 128], IDT[:, :])
                ttick()
                V.wait_ge(TE.s, TE.n)
                V.tensor_copy(SCR[:, 0:512], PST[:, 0:512])
                vtick()
                TN.wait_ge(VE.s, VE.n)
                for w in range(4):
                    TN.matmul(PSE[:, w * 128:(w + 1) * 128], BDH[:, :],
                              SCR[:, w * 128:(w + 1) * 128], start=True, stop=True)
                ttick()
                V.wait_ge(TE.s, TE.n)
                for w in range(4):
                    V.tensor_tensor(FMX[:, :], FMX[:, :], PSE[:, w * 128:(w + 1) * 128], ALU.max)
                    V.tensor_tensor(FMN[:, :], FMN[:, :], PSE[:, w * 128:(w + 1) * 128], ALU.min)
                V.tensor_copy(FBv[:, q0 * 128:(q0 + 4) * 128], PSE[:, 0:512])
                vtick()
            TN.wait_ge(VE.s, VE.n)
            V.tensor_scalar(FMN[:, :], FMN[:, :], -1.0, None, ALU.mult)
            V.tensor_tensor(FMX[:, :], FMX[:, :], FMN[:, :], ALU.max)
            vtick()
            TN.wait_ge(VE.s, VE.n)
            TN.transpose(PST[:, 0:128], FMX[:, :], IDT[:, :])
            ttick()
            V.wait_ge(TE.s, TE.n)
            V.reduce_max(SC1[:, 0:1], PST[:, 0:128], AX.X)
            V.tensor_scalar(SC1[:, 0:1], SC1[:, 0:1], ALPHA, EPS_FFN, ALU.mult, ALU.max)
            V.reciprocal(SC2[:, 0:1], SC1[:, 0:1])
            V.tensor_scalar(SC2[:, 0:1], SC2[:, 0:1], 128.0 * ALPHA, None, ALU.mult)
            V.tensor_scalar(SC1[:, 1:2], SC1[:, 0:1], MGs[:, 2:3], 1.0 / 256.0, ALU.mult, ALU.mult)
            vtick()
            TN.wait_ge(VE.s, VE.n)
            bcast_col(BCC[:, tf * 128:(tf + 1) * 128], SC2[:, 0:1])
            bcast_col(BCD[:, tf * 128:(tf + 1) * 128], SC1[:, 1:2])
            for q in range(64):
                V.tensor_tensor(EV0[:, 0:128], FBv[:, q * 128:(q + 1) * 128],
                                BCC[:, tf * 128:(tf + 1) * 128], ALU.mult)
                V.tensor_scalar(EV0[:, 0:128], EV0[:, 0:128], MAGIC, MAGIC, ALU.add, ALU.subtract)
                V.tensor_scalar(KD[:, q * 512 + tf * 128:q * 512 + tf * 128 + 128],
                                EV0[:, 0:128], 127.0, None, ALU.min)
            vtick()
        for o2 in range(16):
            SY.wait_ge(VE.s, VE.n)
            for seg in range(8):
                ld(ARA[:, seg * 1024:(seg + 1) * 1024].rearrange("p (a c) -> p a c", a=8),
                   WdP2[o2, seg * 1024:(seg + 1) * 1024, :].rearrange("(a p) c -> p a c", p=128))
            A.wait_ge(LD.s, LD.n)
            for gc in range(8):
                if gc >= 1:
                    A.wait_ge(VE.s, VE.n)
                A.activation(WS2[:, 0:1024], ARA[:, gc * 1024:(gc + 1) * 1024],
                             AF.Sign, bias=BN5[:, :], scale=MGi[:, 2:3])
                A.activation(KQR[:, 0:1024], ARA[:, gc * 1024:(gc + 1) * 1024],
                             AF.Sign, bias=BP5[:, :], scale=MGi[:, 2:3])
                atick()
                V.wait_ge(AC.s, AC.n)
                V.tensor_tensor(CUt[:, gc * 1024:(gc + 1) * 1024],
                                WS2[:, 0:1024], KQR[:, 0:1024], ALU.add)
                vtick()
            SY.wait_ge(VE.s, VE.n)
            ld(WF0[:, 0:512], xT[o2 * 128:(o2 + 1) * 128, th * 512:th * 512 + 512])
            ld(WF1[:, 0:512], sH[o2 * 128:(o2 + 1) * 128, th * 512:th * 512 + 512])
            TN.wait_ge(VE.s, VE.n)
            for g in range(64):
                TN.matmul(PSA[:, 0:512], CUt[:, g * 128:(g + 1) * 128],
                          KD[:, g * 512:g * 512 + 512], start=(g == 0), stop=(g == 63))
            ttick()
            V.wait_ge(TE.s, TE.n)
            V.tensor_tensor(EV0[:, :], PSA[:, 0:512], BCD[:, 0:512], ALU.mult)
            vtick()
            A.wait_ge(VE.s, VE.n)
            A.activation(EV1[:, :], EV0[:, :], AF.Silu)
            atick()
            V.wait_ge(AC.s, AC.n)
            V.wait_ge(LD.s, LD.n)
            V.tensor_tensor(EV2[:, :], EV1[:, :], WF1[:, 0:512], ALU.mult)
            V.tensor_tensor(EV2[:, :], EV2[:, :], WF0[:, 0:512], ALU.add)
            vtick()
            A.wait_ge(VE.s, VE.n)
            A.dma_start(outT[o2 * 128:(o2 + 1) * 128, th * 512:th * 512 + 512],
                        EV2[:, :]).then_inc(STS.s, 16)
            STS.n += 16
            A.wait_ge(STS.s, STS.n)
            TN.wait_ge(VE.s, VE.n)
    barrier()
    A.wait_ge(STS.s, STS.n)
    return nc


_NC_CACHE = {}


def kernel(L0_attn_x, L0_attn_slope_rate, L0_attn_W_qkvu, L0_attn_W_out,
           L0_ffn_w_gate, L0_ffn_w_up, L0_ffn_w_down):
    x = np.asarray(L0_attn_x, np.float32)[0]
    slope = np.asarray(L0_attn_slope_rate, np.float32)[:, 0, 0]
    WT = np.ascontiguousarray(np.asarray(L0_attn_W_qkvu, np.float32).T)
    WoT = np.ascontiguousarray(np.asarray(L0_attn_W_out, np.float32).T)
    WgTh = np.ascontiguousarray(np.asarray(L0_ffn_w_gate, np.float32).T)
    WuTh = np.ascontiguousarray(np.asarray(L0_ffn_w_up, np.float32).T)
    WdT = np.ascontiguousarray(np.asarray(L0_ffn_w_down, np.float32).T)
    WdTp = WdT[_fwht_perm(), :]
    WdP2 = np.ascontiguousarray(WdTp.reshape(8192, 16, 128).transpose(1, 0, 2))
    Wq4 = np.ascontiguousarray(WT.reshape(16, 128, 64, 128).transpose(2, 1, 0, 3)
                               .reshape(64, 128, 2048))
    Wo4 = np.ascontiguousarray(WoT.reshape(16, 128, 16, 128).transpose(2, 1, 0, 3)
                               .reshape(16, 128, 2048))
    H128 = _hadamard(128).astype(BFD)
    H64 = _hadamard(64)
    BD64 = np.zeros((128, 128), np.float32)
    BD64[:64, :64] = H64
    BD64[64:, 64:] = H64
    p = np.arange(128, dtype=np.float32)[:, None]
    consts = dict(
        H128c=H128, BD64c=BD64.astype(BFD),
        IDb=np.eye(128, dtype=np.float32).astype(BFD),
        ONESTc=np.ones((128, 128), np.float32),
        SCONEc=np.full((128, 128), 1.0 / 2048.0, np.float32).astype(BFD),
        ONESBc=np.ones((128, 128), np.float32).astype(BFD),
        ARR1=np.broadcast_to(np.arange(1, 257, dtype=np.float32)[None, :], (128, 256)).copy(),
        IDXD=np.concatenate([
            np.where(np.arange(256)[None, :] - (sh * 128 + p) >= 0,
                     np.arange(256, dtype=np.float32)[None, :] - (sh * 128 + p),
                     1e5).astype(np.float32)
            for sh in range(2)], axis=1),
        ARRC=np.concatenate([1023.0 - (tt * 128 + p) for tt in range(8)], axis=1).astype(np.float32),
        ARRB=np.concatenate([255.0 - (sh * 128 + p) for sh in range(2)], axis=1).astype(np.float32),
        sl_col=np.broadcast_to(slope[None, :], (128, 16)).copy(),
        Wqkvu=WT, Wq4=Wq4, Wo4=Wo4, WgT=WgTh, WuT=WuTh, WdP2=WdP2,
    )
    xall = np.ascontiguousarray(x.T)
    in_maps = []
    for c in range(8):
        xprev = np.zeros((2048, 2048), np.float32)
        if c >= 2:
            xprev[:, 0:1024] = xall[:, (c - 2) * 1024:(c - 1) * 1024]
        if c >= 1:
            xprev[:, 1024:2048] = xall[:, (c - 1) * 1024:c * 1024]
        m = dict(consts)
        m["xT"] = np.ascontiguousarray(xall[:, c * 1024:(c + 1) * 1024])
        m["xprevT"] = xprev
        in_maps.append(m)
    key = os.environ.get("KPHASES", "99")
    if key not in _NC_CACHE:
        _NC_CACHE[key] = build()
    res = run_bass_kernel_spmd(_NC_CACHE[key], in_maps, core_ids=list(range(8)))
    out = np.concatenate([res.results[c]["outT"].T for c in range(8)], axis=0)
    return np.ascontiguousarray(out, dtype=np.float32).reshape(1, 8192, 2048)


# version 08
# speedup vs baseline: 1.1376x; 1.1279x over previous
import os
import numpy as np
import ml_dtypes
import concourse.bass as bass
import concourse.mybir as mybir
from concourse.bass_utils import run_bass_kernel_spmd

F32 = mybir.dt.float32
BF16 = mybir.dt.bfloat16
AF = mybir.ActivationFunctionType
ALU = mybir.AluOpType
AX = mybir.AxisListType
BFD = ml_dtypes.bfloat16
MAGIC = 12582912.0
ALPHA = float(1.0 / np.sqrt(np.float32(8192.0)))

T = 1024
D = 2048
H = 16
EPS_LN = 1e-5
EPS_RMS = 1e-6
EPS_FFN = 1e-5
NWINV = float(1.0 / (D * 4 * D))


def _hadamard(n):
    h = np.array([[1.0]], np.float32)
    while h.shape[0] < n:
        h = np.block([[h, h], [h, -h]]).astype(np.float32)
    return h


def _fwht_perm():
    q = np.arange(64)[:, None]
    p3 = np.arange(128)[None, :]
    return ((p3 % 64) * 128 + 2 * q + p3 // 64).reshape(-1)


def build():
    PH = int(os.environ.get("KPHASES", "99"))
    nc = bass.Bass("TRN2", target_bir_lowering=False,
                   detect_race_conditions=False)
    dr = lambda n, s, dt, k: nc.dram_tensor(n, s, dt, kind=k)
    xT = dr("xT", [D, T], F32, "ExternalInput")
    xprevT = dr("xprevT", [D, 2 * T], F32, "ExternalInput")
    Wqkvu = dr("Wqkvu", [D, 4 * D], F32, "ExternalInput")
    Wq4 = dr("Wq4", [64, 128, 2048], F32, "ExternalInput")
    Wo4 = dr("Wo4", [16, 128, 2048], F32, "ExternalInput")
    WgT = dr("WgT", [D, 4 * D], F32, "ExternalInput")
    WuT = dr("WuT", [D, 4 * D], F32, "ExternalInput")
    WdP2 = dr("WdP2", [16, 4 * D, 128], F32, "ExternalInput")
    sl_col = dr("sl_col", [128, H], F32, "ExternalInput")
    H128c = dr("H128c", [128, 128], BF16, "ExternalInput")
    BD64c = dr("BD64c", [128, 128], BF16, "ExternalInput")
    IDb = dr("IDb", [128, 128], BF16, "ExternalInput")
    ONESTc = dr("ONESTc", [128, 128], F32, "ExternalInput")
    SCONEc = dr("SCONEc", [128, 128], BF16, "ExternalInput")
    ONESBc = dr("ONESBc", [128, 128], BF16, "ExternalInput")
    ARR1 = dr("ARR1", [128, 256], F32, "ExternalInput")
    IDXD = dr("IDXD", [128, 512], F32, "ExternalInput")
    ARRC = dr("ARRC", [128, 8], F32, "ExternalInput")
    ARRB = dr("ARRB", [128, 2], F32, "ExternalInput")
    kqH = dr("kqH", [D, T], BF16, "Internal")
    oH = dr("oH", [D, T], BF16, "Internal")
    sH = dr("sH", [D, T], F32, "Internal")
    interH = dr("interH", [4 * D, T], F32, "Internal")
    outT = dr("outT", [D, T], F32, "ExternalOutput")

    sb = lambda n, s, dt: nc.alloc_sbuf_tensor(n, s, dt)
    ARA = sb("ARA", [128, 8192], F32)
    ARB = sb("ARB", [128, 32768], BF16)
    ARC = sb("ARC", [128, 16384], BF16)
    ARD = sb("ARD", [128, 16384], BF16)
    WS2 = sb("WS2", [128, 1024], BF16)
    WF0 = sb("WF0", [128, 512], F32)
    WF1 = sb("WF1", [128, 512], F32)
    EV0 = sb("EV0", [128, 512], F32)
    EV1 = sb("EV1", [128, 512], F32)
    EV2 = sb("EV2", [128, 512], F32)
    BCC = sb("BCC", [128, 1024], F32)
    BCD = sb("BCD", [128, 1024], F32)
    KV = sb("KV", [128, H * 128], F32)
    KVB = sb("KVB", [128, 128], BF16)
    QDB = sb("QDB", [128, 256], BF16)
    KDm = sb("KDm", [128, 256], BF16)
    KQR = sb("KQR", [128, 1024], BF16)
    FMX = sb("FMX", [128, 128], BF16)
    FMN = sb("FMN", [128, 128], BF16)
    HADt = sb("HADt", [128, 128], BF16)
    BDH = sb("BDH", [128, 128], BF16)
    IDT = sb("IDT", [128, 128], BF16)
    ONEST = sb("ONEST", [128, 128], F32)
    SCONE = sb("SCONE", [128, 128], BF16)
    ONESB = sb("ONESB", [128, 128], BF16)
    AR1 = sb("AR1", [128, 256], F32)
    IDX = sb("IDX", [128, 512], F32)
    ARC_ = sb("ARCt", [128, 8], F32)
    ARBt = sb("ARBt", [128, 2], F32)
    SLC = sb("SLC", [128, H], F32)
    SC0 = sb("SC0", [128, 1], F32)
    SC1 = sb("SC1", [128, 8], F32)
    SC2 = sb("SC2", [128, 8], F32)
    MGc = sb("MGc", [128, 6], F32)
    MGs = sb("MGs", [128, 3], F32)
    MGi = sb("MGi", [128, 3], F32)
    BP5 = sb("BP5", [128, 1], F32)
    BN5 = sb("BN5", [128, 1], F32)
    GSC = sb("GSC", [128, 3], F32)
    BDCOL = sb("BDCOL", [128, H], F32)
    W1COL = sb("W1COL", [128, H], F32)
    KDALL = sb("KDALL", [128, 32], F32)

    PSA = nc.alloc_psum_tensor("PSA", [128, 1024], F32)
    PSB = nc.alloc_psum_tensor("PSB", [128, 1024], F32)
    PSC = nc.alloc_psum_tensor("PSC", [128, 512], F32)
    PSD = nc.alloc_psum_tensor("PSD", [128, 512], F32)
    PSE = nc.alloc_psum_tensor("PSE", [128, 512], F32)
    PST = nc.alloc_psum_tensor("PST", [128, 1024], BF16)

    WS0 = ARA[:, 0:1024].bitcast(BF16)
    WS1 = ARA[:, 1024:2048].bitcast(BF16)
    WS3 = ARA[:, 7680:8192].bitcast(BF16)
    BCA = BCC[:, 0:512]
    BCB = BCD[:, 0:512]

    class Ctr:
        def __init__(self, name):
            self.s = nc.alloc_semaphore(name)
            self.n = 0
    LD, GW, TE, VE, AC, STS, BAR, GW0, GW1 = (Ctr(x) for x in
                                    ("LD", "GW", "TE", "VE", "AC", "STS", "BAR",
                                     "GW0", "GW1"))
    GWP = (GW0, GW1)
    OST = tuple(Ctr(f"OST{i}") for i in range(4))
    STG_S = tuple(Ctr(f"STG{i}") for i in range(3))
    _DRAIN_OPS = {"tensor_copy", "tensor_tensor", "tensor_scalar", "tensor_add",
                  "tensor_sub", "tensor_mul", "tensor_max", "scalar_tensor_tensor",
                  "tensor_reduce", "reduce_sum", "reduce_max", "reciprocal",
                  "memset", "activation", "tensor_single_scalar", "tensor_relu"}

    class _DW:
        def __init__(self, eng):
            self._e = eng
            self.last = None
            self._hist = []

        @staticmethod
        def _names(args, kwargs):
            out = set()
            for x in list(args) + list(kwargs.values()):
                t = getattr(x, "tensor", None)
                n = getattr(t, "name", None)
                if n:
                    out.add(n)
            return out

        def __getattr__(self, name):
            fn = getattr(self._e, name)
            if name in _DRAIN_OPS:
                def g(*a, **k):
                    touched = self._names(a, k)
                    outn = self._names(a[:1], {})
                    if any(touched & h for h in self._hist):
                        self._e.drain()
                        self._hist = []
                    i = fn(*a, **k)
                    self._hist = (self._hist + [outn])[-2:]
                    self.last = i
                    return i
                return g
            if name == "wait_ge":
                def w(*a, **k):
                    self.last = None
                    return fn(*a, **k)
                return w
            return fn

    V, A = _DW(nc.vector), _DW(nc.scalar)
    TN = _DW(nc.tensor)
    _TN_OPS = {"matmul", "transpose", "ldweights"}
    class _TNW:
        def __init__(self, eng):
            self._e = eng
            self.last = None
        def __getattr__(self, name):
            fn = getattr(self._e, name)
            if name in _TN_OPS:
                def g(*a, **k):
                    i = fn(*a, **k)
                    self.last = i
                    return i
                return g
            if name == "wait_ge":
                def w(*a, **k):
                    self.last = None
                    return fn(*a, **k)
                return w
            return fn
    TN = _TNW(nc.tensor)
    SY, GP = nc.sync, nc.gpsimd
    ALLE = (nc.tensor, nc.vector, nc.scalar, SY, GP)

    def ld(dst, src):
        SY.dma_start(dst, src).then_inc(LD.s, 16)
        LD.n += 16

    def gld(dst, src):
        GP.dma_start(dst, src).then_inc(GW.s, 16)
        GW.n += 16

    def gldp(par, dst, src):
        s = GWP[par]
        GP.dma_start(dst, src).then_inc(s.s, 16)
        s.n += 16

    def vtick():
        (V.last if V.last is not None else V._e.nop()).then_inc(VE.s, 1)
        V.last = None
        VE.n += 1

    def atick():
        (A.last if A.last is not None else A._e.nop()).then_inc(AC.s, 1)
        A.last = None
        AC.n += 1

    def ttick():
        (TN.last if TN.last is not None else TN._e.nop()).then_inc(TE.s, 1)
        TN.last = None
        TE.n += 1

    def barrier():
        for e in (nc.tensor, nc.vector, nc.scalar):
            e.drain().then_inc(BAR.s, 1)
        for e in (SY, GP):
            e.nop().then_inc(BAR.s, 1)
        BAR.n += len(ALLE)
        t = BAR.n
        for e in ALLE:
            e.wait_ge(BAR.s, t)

    def tail():
        A.dma_start(outT[0:128, 0:512], EV2[:, :]).then_inc(STS.s, 16)
        STS.n += 16
        A.wait_ge(STS.s, STS.n)

    def exp_tile(dst128, hcol, colap, sgn):
        V.tensor_scalar(dst128, ONEST[:, :], SLC[:, hcol:hcol + 1], colap, ALU.mult, ALU.mult)
        vtick()
        A.wait_ge(VE.s, VE.n)
        A.activation(dst128, dst128, AF.Exp, scale=sgn)
        atick()
        V.wait_ge(AC.s, AC.n)

    for dst, src in ((HADt, H128c), (BDH, BD64c), (IDT, IDb),
                     (ONEST, ONESTc), (SCONE, SCONEc), (ONESB, ONESBc),
                     (AR1, ARR1), (IDX, IDXD),
                     (ARC_, ARRC), (ARBt, ARRB), (SLC, sl_col)):
        ld(dst[:, :], src[:, :])
    V.wait_ge(LD.s, LD.n)
    V.memset(GSC[:, :], 0.0)
    V.tensor_scalar(BDCOL[:, :], SLC[:, :], -256.0, None, ALU.mult)
    V.tensor_scalar(W1COL[:, :], SLC[:, :], -1024.0, None, ALU.mult)
    for sh in range(2):
        V.tensor_scalar(KDALL[:, sh * 16:(sh + 1) * 16], SLC[:, :],
                        ARBt[:, sh:sh + 1], -1.0, ALU.mult, ALU.mult)
    vtick()
    A.wait_ge(VE.s, VE.n)
    A.activation(BDCOL[:, :], BDCOL[:, :], AF.Exp)
    A.activation(W1COL[:, :], W1COL[:, :], AF.Exp)
    A.activation(KDALL[:, :], KDALL[:, :], AF.Exp)
    atick()
    V.wait_ge(AC.s, AC.n)
    vtick()
    barrier()
    if PH <= 1:
        tail(); return nc

    def ln(xdram, col0, dst, dstcol0, dststride):
        for j in range(16):
            ld(ARA[:, j * 512:(j + 1) * 512], xdram[j * 128:(j + 1) * 128, col0:col0 + 512])
        TN.wait_ge(LD.s, LD.n)
        V.wait_ge(LD.s, LD.n)
        A.wait_ge(LD.s, LD.n)
        for j in range(16):
            if j > 0:
                V.wait_ge(TE.s, TE.n)
            V.tensor_copy(WS2[:, 0:512], ARA[:, j * 512:(j + 1) * 512])
            V.tensor_tensor(EV0[:, :], ARA[:, j * 512:(j + 1) * 512], WS2[:, 0:512], ALU.subtract)
            V.tensor_copy(WS2[:, 512:1024], EV0[:, :])
            vtick()
            TN.wait_ge(VE.s, VE.n)
            TN.matmul(PSC[:, :], SCONE[:, :], WS2[:, 0:512], start=(j == 0), stop=False)
            TN.matmul(PSC[:, :], SCONE[:, :], WS2[:, 512:1024], start=False,
                      stop=(j == 15))
            ttick()
            A.wait_ge(TE.s, TE.n)
            A.activation(WS2[:, 0:512], ARA[:, j * 512:(j + 1) * 512], AF.Square)
            atick()
            TN.wait_ge(AC.s, AC.n)
            TN.matmul(PSD[:, :], SCONE[:, :], WS2[:, 0:512], start=(j == 0), stop=(j == 15))
            ttick()
        V.wait_ge(TE.s, TE.n)
        V.tensor_copy(BCA[:, :], PSC[:, :])
        V.scalar_tensor_tensor(EV0[:, :], BCA[:, :], 1.0, BCA[:, :], ALU.bypass, ALU.mult)
        V.tensor_copy(EV1[:, :], PSD[:, :])
        V.tensor_tensor(EV1[:, :], EV1[:, :], EV0[:, :], ALU.subtract)
        V.tensor_scalar(EV1[:, :], EV1[:, :], EPS_LN, None, ALU.add)
        vtick()
        TN.wait_ge(VE.s, VE.n)
        A.wait_ge(VE.s, VE.n)
        A.activation(EV0[:, :], EV1[:, :], AF.Sqrt)
        atick()
        V.wait_ge(AC.s, AC.n)
        V.reciprocal(EV0[:, :], EV0[:, :])
        V.scalar_tensor_tensor(EV2[:, :], EV0[:, :], 1.0, EV0[:, :], ALU.bypass, ALU.mult)
        V.tensor_tensor(EV2[:, :], EV2[:, :], EV1[:, :], ALU.mult)
        V.tensor_scalar(EV2[:, :], EV2[:, :], -0.5, 1.5, ALU.mult, ALU.add)
        V.tensor_tensor(BCB[:, :], EV0[:, :], EV2[:, :], ALU.mult)
        for j in range(16):
            V.tensor_tensor(ARA[:, j * 512:(j + 1) * 512], ARA[:, j * 512:(j + 1) * 512],
                            BCA[:, :], ALU.subtract)
            V.tensor_tensor(dst[:, j * dststride + dstcol0:j * dststride + dstcol0 + 512],
                            ARA[:, j * 512:(j + 1) * 512], BCB[:, :], ALU.mult)
        vtick()
        SY.wait_ge(VE.s, VE.n)

    def bcast_col(dst128, colap):
        V.tensor_scalar(EV0[:, 0:128], ONEST[:, :], colap, None, ALU.mult)
        V.tensor_copy(WS2[:, 0:128], EV0[:, 0:128])
        V.tensor_tensor(EV1[:, 0:128], EV0[:, 0:128], WS2[:, 0:128], ALU.subtract)
        V.tensor_copy(WS2[:, 128:256], EV1[:, 0:128])
        vtick()
        TN.wait_ge(VE.s, VE.n)
        TN.matmul(PSE[:, 0:128], WS2[:, 0:128], IDT[:, :], start=True, stop=False)
        TN.matmul(PSE[:, 0:128], WS2[:, 128:256], IDT[:, :], start=False, stop=True)
        ttick()
        V.wait_ge(TE.s, TE.n)
        V.tensor_copy(dst128, PSE[:, 0:128])
        vtick()
        TN.wait_ge(VE.s, VE.n)

    for pc in range(2):
        ln(xprevT, pc * 1024, ARB, 0, 1024)
        if PH == 11:
            barrier(); tail(); return nc
        ln(xprevT, pc * 1024 + 512, ARB, 512, 1024)
        barrier()
        if PH == 12:
            tail(); return nc
        CGp = ARD[:, 0:8192]
        CUp = ARD[:, 8192:16384]
        for obp in range(4):
            kcol = 2048 + obp * 512
            vcol = 4096 + obp * 512
            GP.wait_ge(TE.s, TE.n)
            for hh in range(16):
                gld(CGp[:, hh * 512:(hh + 1) * 512],
                    Wqkvu[hh * 128:(hh + 1) * 128, kcol:kcol + 512])
                gld(CUp[:, hh * 512:(hh + 1) * 512],
                    Wqkvu[hh * 128:(hh + 1) * 128, vcol:vcol + 512])
            TN.wait_ge(GW.s, GW.n)
            for tt in range(8):
                for hl in range(4):
                    V.tensor_scalar(ARA[:, tt * 512 + hl * 128:tt * 512 + hl * 128 + 128],
                                    ONEST[:, :], SLC[:, obp * 4 + hl:obp * 4 + hl + 1],
                                    ARC_[:, tt:tt + 1], ALU.mult, ALU.mult)
            vtick()
            A.wait_ge(VE.s, VE.n)
            A.activation(ARA[:, 0:4096], ARA[:, 0:4096], AF.Exp, scale=-1.0)
            atick()
            V.wait_ge(AC.s, AC.n)
            PAIRS = ((PSA[:, 0:512], PSA[:, 512:1024]),
                     (PSB[:, 0:512], PSB[:, 512:1024]),
                     (PSC[:, :], PSD[:, :]))
            pair_ve = [0, 0, 0]
            pair_ac = [0, 0, 0]
            sacc_ve = 0
            SACC = BCC[:, 0:512]
            for tt in range(8):
                pk, pv = PAIRS[tt % 3]
                w2o = (tt % 2) * 512
                TN.wait_ge(VE.s, pair_ve[tt % 3])
                TN.wait_ge(AC.s, pair_ac[tt % 3])
                for hh in range(16):
                    TN.matmul(pk, ARB[:, hh * 1024 + tt * 128:hh * 1024 + tt * 128 + 128],
                              CGp[:, hh * 512:(hh + 1) * 512], start=(hh == 0), stop=(hh == 15))
                    TN.matmul(pv, ARB[:, hh * 1024 + tt * 128:hh * 1024 + tt * 128 + 128],
                              CUp[:, hh * 512:(hh + 1) * 512], start=(hh == 0), stop=(hh == 15))
                ttick()
                A.wait_ge(TE.s, TE.n)
                A.wait_ge(VE.s, sacc_ve)
                A.activation(WS2[:, w2o:w2o + 512], pk, AF.Silu)
                atick()
                pair_ac[tt % 3] = AC.n
                V.wait_ge(TE.s, TE.n)
                V.tensor_copy(WS3[:, w2o:w2o + 512], pv)
                V.wait_ge(AC.s, AC.n)
                V.tensor_tensor(WS2[:, w2o:w2o + 512], WS2[:, w2o:w2o + 512],
                                ARA[:, tt * 512:(tt + 1) * 512], ALU.mult)
                vtick()
                pair_ve[tt % 3] = VE.n
                TN.wait_ge(VE.s, VE.n)
                for hl in range(4):
                    TN.matmul(PSE[:, hl * 128:(hl + 1) * 128],
                              WS2[:, w2o + hl * 128:w2o + hl * 128 + 128],
                              WS3[:, w2o + hl * 128:w2o + hl * 128 + 128],
                              start=True, stop=True)
                ttick()
                V.wait_ge(TE.s, TE.n)
                if tt == 0:
                    V.tensor_copy(SACC[:, :], PSE[:, :])
                else:
                    V.tensor_tensor(SACC[:, :], SACC[:, :], PSE[:, :], ALU.add)
                vtick()
                sacc_ve = VE.n
                TN.wait_ge(VE.s, sacc_ve)
            V.wait_ge(TE.s, TE.n)
            if pc == 0:
                V.tensor_copy(KV[:, obp * 512:(obp + 1) * 512], SACC[:, :])
                vtick()
            else:
                for hl in range(4):
                    h = obp * 4 + hl
                    V.tensor_scalar(KV[:, h * 128:(h + 1) * 128], KV[:, h * 128:(h + 1) * 128],
                                    W1COL[:, h:h + 1], None, ALU.mult)
                    V.tensor_tensor(KV[:, h * 128:(h + 1) * 128], KV[:, h * 128:(h + 1) * 128],
                                    SACC[:, hl * 128:(hl + 1) * 128], ALU.add)
                vtick()
            TN.wait_ge(VE.s, VE.n)
            if PH == 13:
                barrier(); tail(); return nc
        barrier()
        if PH <= 2 + pc:
            tail(); return nc
    barrier()

    ln(xT, 0, ARC, 0, 1024)
    ln(xT, 512, ARC, 512, 1024)
    barrier()
    XN = ARC
    V.tensor_tensor(BCD[:, :], XN[:, 0:1024], XN[:, 1024:2048], ALU.max)
    V.tensor_tensor(BCC[:, 0:1024], XN[:, 0:1024], XN[:, 1024:2048], ALU.min)
    for j in range(2, 16):
        V.tensor_tensor(BCD[:, :], BCD[:, :], XN[:, j * 1024:(j + 1) * 1024], ALU.max)
        V.tensor_tensor(BCC[:, 0:1024], BCC[:, 0:1024], XN[:, j * 1024:(j + 1) * 1024], ALU.min)
    V.tensor_scalar(BCC[:, 0:1024], BCC[:, 0:1024], -1.0, None, ALU.mult)
    V.tensor_tensor(BCD[:, :], BCD[:, :], BCC[:, 0:1024], ALU.max)
    V.tensor_copy(WS2[:, 0:1024], BCD[:, :])
    vtick()
    TN.wait_ge(VE.s, VE.n)
    for tt in range(8):
        TN.transpose(PST[:, 0:128], WS2[:, tt * 128:(tt + 1) * 128], IDT[:, :])
        ttick()
        V.wait_ge(TE.s, TE.n)
        V.reduce_max(SC1[:, tt:tt + 1], PST[:, 0:128], AX.X)
        vtick()
        TN.wait_ge(VE.s, VE.n)
    V.tensor_scalar(SC1[:, 0:8], SC1[:, 0:8], EPS_FFN, None, ALU.max)
    V.reciprocal(SC2[:, 0:8], SC1[:, 0:8])
    V.tensor_scalar(SC2[:, 0:8], SC2[:, 0:8], 128.0, None, ALU.mult)
    vtick()
    for tt in range(8):
        bcast_col(BCC[:, tt * 128:(tt + 1) * 128], SC2[:, tt:tt + 1])
    for j in range(16):
        buf = WS2 if j % 2 == 0 else WS3
        if j >= 2:
            V.wait_ge(LD.s, LD.n)
        V.tensor_tensor(EV0[:, :], XN[:, j * 1024:j * 1024 + 512], BCC[:, 0:512], ALU.mult)
        V.tensor_tensor(EV1[:, :], XN[:, j * 1024 + 512:(j + 1) * 1024], BCC[:, 512:1024], ALU.mult)
        V.tensor_scalar(EV0[:, :], EV0[:, :], MAGIC, MAGIC, ALU.add, ALU.subtract)
        V.tensor_scalar(EV1[:, :], EV1[:, :], MAGIC, MAGIC, ALU.add, ALU.subtract)
        V.tensor_scalar(buf[:, 0:512], EV0[:, :], 127.0, None, ALU.min)
        V.tensor_scalar(buf[:, 512:1024], EV1[:, :], 127.0, None, ALU.min)
        vtick()
        SY.wait_ge(VE.s, VE.n)
        ld(kqH[j * 128:(j + 1) * 128, :], buf[:, 0:1024])
    V.wait_ge(LD.s, LD.n)
    barrier()
    if PH <= 4:
        tail(); return nc

    scr = lambda a, b: ARA[:, a:b]
    for HF in range(2):
        th0 = HF * 512
        for oc in range(32):
            buf = WS0 if oc % 2 == 0 else WS1
            if oc >= 2:
                GP.wait_ge(TE.s, TE.n)
            gldp(oc % 2, buf[:, :], Wq4[oc, :, :])
            TN.wait_ge(GWP[oc % 2].s, GWP[oc % 2].n)
            ps = PSA[:, 0:512] if oc % 2 == 0 else PSA[:, 512:1024]
            if oc >= 2:
                TN.wait_ge(AC.s, AC.n)
            for hh in range(16):
                TN.matmul(ps, buf[:, hh * 128:(hh + 1) * 128],
                          XN[:, hh * 1024 + th0:hh * 1024 + th0 + 512],
                          start=(hh == 0), stop=(hh == 15))
            ttick()
            A.wait_ge(TE.s, TE.n)
            if oc < 16:
                dst = ARB[:, oc * 512:(oc + 1) * 512]
            else:
                dst = ARB[:, 8192 + (oc - 16) * 512:8192 + (oc - 15) * 512]
            A.activation(dst, ps, AF.Silu)
            atick()
        for oc in range(48, 64):
            i = oc - 48
            buf = WS0 if i % 2 == 0 else WS1
            GP.wait_ge(TE.s, TE.n)
            gldp(i % 2, buf[:, :], Wq4[oc, :, :])
            TN.wait_ge(GWP[i % 2].s, GWP[i % 2].n)
            ps = PSB[:, 0:512] if i % 2 == 0 else PSB[:, 512:1024]
            if i >= 2:
                TN.wait_ge(VE.s, VE.n)
            for hh in range(16):
                TN.matmul(ps, buf[:, hh * 128:(hh + 1) * 128],
                          XN[:, hh * 1024 + th0:hh * 1024 + th0 + 512],
                          start=(hh == 0), stop=(hh == 15))
            ttick()
            V.wait_ge(TE.s, TE.n)
            V.tensor_copy(ARD[:, i * 1024 + th0:i * 1024 + th0 + 512], ps)
            vtick()
        A.wait_ge(TE.s, TE.n)
        V.wait_ge(TE.s, TE.n)
        TN.wait_ge(AC.s, AC.n)
        for ob in range(4):
            for hh in range(16):
                buf = WS0 if hh % 2 == 0 else WS1
                GP.wait_ge(TE.s, TE.n)
                gldp(hh % 2, buf[:, 0:512],
                     Wqkvu[hh * 128:(hh + 1) * 128, 4096 + ob * 512:4096 + ob * 512 + 512])
                TN.wait_ge(GWP[hh % 2].s, GWP[hh % 2].n)
                if hh == 0:
                    TN.wait_ge(VE.s, VE.n)
                for tl in range(4):
                    bank = (PSA[:, 0:512], PSA[:, 512:1024], PSB[:, 0:512], PSB[:, 512:1024])[tl]
                    TN.matmul(bank, XN[:, hh * 1024 + th0 + tl * 128:hh * 1024 + th0 + tl * 128 + 128],
                              buf[:, 0:512], start=(hh == 0), stop=(hh == 15))
                ttick()
            V.wait_ge(TE.s, TE.n)
            for tl in range(4):
                bank = (PSA[:, 0:512], PSA[:, 512:1024], PSB[:, 0:512], PSB[:, 512:1024])[tl]
                V.tensor_copy(ARB[:, 16384 + tl * 2048 + ob * 512:16384 + tl * 2048 + ob * 512 + 512], bank)
            vtick()
        TN.wait_ge(VE.s, VE.n)
        V.wait_ge(TE.s, TE.n)
        QDV = (BCC[:, :].bitcast(BF16), BCD[:, :].bitcast(BF16))
        for h in range(16):
            V.tensor_scalar(ARA[:, h * 256:(h + 1) * 256], AR1[:, :],
                            SLC[:, h:h + 1], None, ALU.mult)
        vtick()
        A.wait_ge(VE.s, VE.n)
        A.activation(ARA[:, 0:4096], ARA[:, 0:4096], AF.Exp, scale=-1.0)
        atick()
        V.wait_ge(AC.s, AC.n)
        V.tensor_copy(QDV[0][:, :], ARA[:, 0:2048])
        V.tensor_copy(QDV[1][:, :], ARA[:, 2048:4096])
        vtick()
        for h in range(16):
            V.tensor_scalar(ARA[:, h * 512:h * 512 + 256], IDX[:, 0:256],
                            SLC[:, h:h + 1], None, ALU.mult)
            V.tensor_scalar(ARA[:, h * 512 + 256:h * 512 + 512], IDX[:, 256:512],
                            SLC[:, h:h + 1], None, ALU.mult)
        vtick()
        A.wait_ge(VE.s, VE.n)
        A.activation(ARA[:, 0:8192], ARA[:, 0:8192], AF.Exp, scale=-1.0)
        atick()
        V.wait_ge(AC.s, AC.n)
        vtick()
        for bl in (2 * HF, 2 * HF + 1):
            bloc = (bl % 2) * 256
            tlb0 = 2 * (bl % 2)
            for h in range(16):
                qdv = QDV[h // 8][:, (h % 8) * 256:(h % 8) * 256 + 256]
                V.wait_ge(TE.s, TE.n)
                V.tensor_copy(KVB[:, :], KV[:, h * 128:(h + 1) * 128])
                V.tensor_tensor(QDB[:, :], ARB[:, h * 512 + bloc:h * 512 + bloc + 256],
                                qdv, ALU.mult)
                vtick()
                TN.wait_ge(VE.s, VE.n)
                for sh in range(2):
                    TN.matmul(PSB[:, sh * 256:sh * 256 + 256],
                              ARB[:, 8192 + h * 512 + bloc + sh * 128:8192 + h * 512 + bloc + sh * 128 + 128],
                              ARB[:, h * 512 + bloc:h * 512 + bloc + 256], start=True, stop=True)
                ttick()
                TN.matmul(PSA[:, 0:256], KVB[:, :], QDB[:, :], start=True, stop=False)
                V.wait_ge(TE.s, TE.n)
                V.tensor_tensor(WS2[:, 0:256], PSB[:, 0:256],
                                ARA[:, h * 512:h * 512 + 256], ALU.mult)
                V.tensor_tensor(WS2[:, 256:512], PSB[:, 256:512],
                                ARA[:, h * 512 + 256:h * 512 + 512], ALU.mult)
                vtick()
                TN.wait_ge(VE.s, VE.n)
                for sh in range(2):
                    TN.matmul(PSA[:, 0:256],
                              ARB[:, 16384 + (tlb0 + sh) * 2048 + h * 128:16384 + (tlb0 + sh) * 2048 + h * 128 + 128],
                              WS2[:, sh * 256:sh * 256 + 256], start=False, stop=(sh == 1))
                for sh in range(2):
                    TN.transpose(PST[:, sh * 128:sh * 128 + 128],
                                 ARB[:, 8192 + h * 512 + bloc + sh * 128:8192 + h * 512 + bloc + sh * 128 + 128],
                                 IDT[:, :])
                ttick()
                slot = h % 4
                V.wait_ge(TE.s, TE.n)
                V.wait_ge(OST[slot].s, OST[slot].n)
                V.tensor_copy(KQR[:, slot * 256:slot * 256 + 256], PSA[:, 0:256])
                V.tensor_scalar(KDm[:, 0:128], PST[:, 0:128],
                                KDALL[:, 0 * 16 + h:0 * 16 + h + 1], None, ALU.mult)
                V.tensor_scalar(KDm[:, 128:256], PST[:, 128:256],
                                KDALL[:, 1 * 16 + h:1 * 16 + h + 1], None, ALU.mult)
                vtick()
                SY.wait_ge(VE.s, VE.n)
                SY.dma_start(oH[h * 128:(h + 1) * 128, bl * 256:bl * 256 + 256],
                             KQR[:, slot * 256:slot * 256 + 256]).then_inc(OST[slot].s, 16)
                OST[slot].n += 16
                TN.wait_ge(VE.s, VE.n)
                for sh in range(2):
                    TN.matmul(PSC[:, 0:128], KDm[:, sh * 128:sh * 128 + 128],
                              ARB[:, 16384 + (tlb0 + sh) * 2048 + h * 128:16384 + (tlb0 + sh) * 2048 + h * 128 + 128],
                              start=(sh == 0), stop=(sh == 1))
                ttick()
                V.wait_ge(TE.s, TE.n)
                V.tensor_scalar(KV[:, h * 128:(h + 1) * 128], KV[:, h * 128:(h + 1) * 128],
                                BDCOL[:, h:h + 1], None, ALU.mult)
                V.tensor_tensor(KV[:, h * 128:(h + 1) * 128], KV[:, h * 128:(h + 1) * 128],
                                PSC[:, 0:128], ALU.add)
                vtick()
        barrier()
        if PH <= 5 + HF:
            tail(); return nc

    for s in OST:
        SY.wait_ge(s.s, s.n)
    for j in range(16):
        ld(ARB[:, j * 1024:(j + 1) * 1024], oH[j * 128:(j + 1) * 128, :])
    OTv = ARB
    for j in range(16):
        A.wait_ge(LD.s, LD.n)
        A.wait_ge(TE.s, TE.n)
        A.activation(WS2[:, 0:1024], OTv[:, j * 1024:(j + 1) * 1024], AF.Square)
        atick()
        TN.wait_ge(AC.s, AC.n)
        TN.matmul(PSC[:, :], ONESB[:, :], WS2[:, 0:512], start=(j == 0), stop=(j == 15))
        TN.matmul(PSD[:, :], ONESB[:, :], WS2[:, 512:1024], start=(j == 0), stop=(j == 15))
        ttick()
        A.wait_ge(TE.s, TE.n)
    V.wait_ge(TE.s, TE.n)
    for hf in range(2):
        ps = PSC if hf == 0 else PSD
        V.tensor_scalar(EV1[:, :], ps[:, :], 1.0 / 2048.0, EPS_RMS, ALU.mult, ALU.add)
        vtick()
        A.wait_ge(VE.s, VE.n)
        A.activation(EV0[:, :], EV1[:, :], AF.Sqrt)
        atick()
        V.wait_ge(AC.s, AC.n)
        V.reciprocal(EV0[:, :], EV0[:, :])
        V.scalar_tensor_tensor(EV2[:, :], EV0[:, :], 1.0, EV0[:, :], ALU.bypass, ALU.mult)
        V.tensor_tensor(EV2[:, :], EV2[:, :], EV1[:, :], ALU.mult)
        V.tensor_scalar(EV2[:, :], EV2[:, :], -0.5, 1.5, ALU.mult, ALU.add)
        V.tensor_tensor(BCC[:, hf * 512:hf * 512 + 512], EV0[:, :], EV2[:, :], ALU.mult)
        vtick()
    TN.wait_ge(VE.s, VE.n)
    for j in range(16):
        V.tensor_tensor(WS2[:, 0:1024], ARD[:, j * 1024:(j + 1) * 1024], BCC[:, :], ALU.mult)
        V.tensor_tensor(ARD[:, j * 1024:(j + 1) * 1024], WS2[:, 0:1024],
                        OTv[:, j * 1024:(j + 1) * 1024], ALU.mult)
        vtick()
    barrier()
    for oc in range(16):
        buf = WS0 if oc % 2 == 0 else WS1
        if oc >= 2:
            GP.wait_ge(TE.s, TE.n)
        gldp(oc % 2, buf[:, :], Wo4[oc, :, :])
        TN.wait_ge(GWP[oc % 2].s, GWP[oc % 2].n)
        for th in range(2):
            ps = PSA[:, 0:512] if th == 0 else PSA[:, 512:1024]
            if oc >= 1 or th >= 1:
                TN.wait_ge(VE.s, VE.n)
            for hh in range(16):
                TN.matmul(ps, buf[:, hh * 128:(hh + 1) * 128],
                          ARD[:, hh * 1024 + th * 512:hh * 1024 + th * 512 + 512],
                          start=(hh == 0), stop=(hh == 15))
            ttick()
            V.wait_ge(TE.s, TE.n)
            V.tensor_copy(EV2[:, :], ps)
            vtick()
            SY.wait_ge(VE.s, VE.n)
            ld(sH[oc * 128:(oc + 1) * 128, th * 512:th * 512 + 512], EV2[:, :])
            V.wait_ge(LD.s, LD.n)
    barrier()
    if PH <= 7:
        tail(); return nc

    MB = (ARA[:, 0:4096].bitcast(BF16), ARA[:, 4096:8192].bitcast(BF16))
    for mi, wsrc in enumerate((WgT, WuT, WdP2)):
        for ch in range(16):
            buf = MB[ch % 2]
            if mi * 16 + ch >= 2:
                GP.wait_ge(VE.s, VE.n)
            if mi < 2:
                gldp(ch % 2, buf[:, :], wsrc[ch * 128:(ch + 1) * 128, :])
            else:
                gldp(ch % 2, buf.rearrange("p (a o) -> p a o", a=64),
                     wsrc[ch, :, :].rearrange("(a p) o -> p a o", p=128))
            V.wait_ge(GWP[ch % 2].s, GWP[ch % 2].n)
            if ch == 0:
                V.reduce_sum(GSC[:, mi:mi + 1], buf[:, :], AX.X, apply_absolute_value=True)
            else:
                V.reduce_sum(SC0[:, :], buf[:, :], AX.X, apply_absolute_value=True)
                V.tensor_tensor(GSC[:, mi:mi + 1], GSC[:, mi:mi + 1], SC0[:, :], ALU.add)
            vtick()
    barrier()
    V.tensor_copy(WS2[:, 0:3], GSC[:, 0:3])
    V.tensor_tensor(EV0[:, 0:3], GSC[:, 0:3], WS2[:, 0:3], ALU.subtract)
    V.tensor_copy(WS2[:, 8:11], EV0[:, 0:3])
    vtick()
    TN.wait_ge(VE.s, VE.n)
    TN.matmul(PSE[:, 0:3], ONESB[:, :], WS2[:, 0:3], start=True, stop=False)
    TN.matmul(PSE[:, 0:3], ONESB[:, :], WS2[:, 8:11], start=False, stop=True)
    ttick()
    V.wait_ge(TE.s, TE.n)
    V.tensor_scalar(MGs[:, 0:3], PSE[:, 0:3], NWINV, EPS_FFN, ALU.mult, ALU.max)
    V.tensor_scalar(MGc[:, 0:3], MGs[:, 0:3], 0.5, None, ALU.mult)
    V.tensor_scalar(MGc[:, 3:6], MGs[:, 0:3], -0.5, None, ALU.mult)
    V.reciprocal(MGi[:, 0:3], MGs[:, 0:3])
    V.memset(BP5[:, :], 0.5)
    V.memset(BN5[:, :], -0.5)
    vtick()
    TN.wait_ge(VE.s, VE.n)
    barrier()
    if PH <= 8:
        tail(); return nc

    for j in range(16):
        ld(ARC[:, j * 1024:(j + 1) * 1024], kqH[j * 128:(j + 1) * 128, :])
    KQ = ARC
    CGt = ARD[:, 0:8192]
    CUt = ARD[:, 8192:16384]
    for tt in range(8):
        bcast_col(BCC[:, tt * 128:(tt + 1) * 128], SC1[:, tt:tt + 1])
    V.tensor_copy(BCD[:, :], BCC[:, :])
    V.tensor_scalar(BCC[:, :], BCC[:, :], MGs[:, 0:1], 1.0 / 256.0, ALU.mult, ALU.mult)
    V.tensor_scalar(BCD[:, :], BCD[:, :], MGs[:, 1:2], -1.0 / 128.0, ALU.mult, ALU.mult)
    vtick()
    V.wait_ge(LD.s, LD.n)
    vtick()
    p5_q_ve = [VE.n]
    p5_q_ac = [AC.n]
    STG = (EV2, WF0, WF1)
    stg_i = 0
    for og in range(16):
        for mat, wsrcT, cdst in ((0, WgT, CGt), (1, WuT, CUt)):
            if os.environ.get("NOPF5"):
                SY.wait_ge(VE.s, VE.n)
                SY.wait_ge(AC.s, AC.n)
            else:
                SY.wait_ge(VE.s, p5_q_ve[0])
                SY.wait_ge(AC.s, p5_q_ac[0])
            ld(ARA[:, :].rearrange("p (a c) -> p a c", a=16),
               wsrcT[:, og * 512:og * 512 + 512].rearrange("(a p) c -> p a c", p=128))
            if mat == 0:
                A.wait_ge(LD.s, LD.n)
                for hh in range(16):
                    buf = WS2 if hh % 2 == 0 else KQR
                    if hh >= 2:
                        A.wait_ge(VE.s, VE.n)
                    A.activation(buf[:, 0:512], ARA[:, hh * 512:(hh + 1) * 512],
                                 AF.Sign, bias=BN5[:, :], scale=MGi[:, 0:1])
                    A.activation(buf[:, 512:1024], ARA[:, hh * 512:(hh + 1) * 512],
                                 AF.Sign, bias=BP5[:, :], scale=MGi[:, 0:1])
                    atick()
                    V.wait_ge(AC.s, AC.n)
                    V.tensor_tensor(cdst[:, hh * 512:(hh + 1) * 512],
                                    buf[:, 0:512], buf[:, 512:1024], ALU.add)
                    vtick()
            else:
                V.wait_ge(LD.s, LD.n)
                for hh in range(16):
                    V.tensor_scalar(WS2[:, 0:512], ARA[:, hh * 512:(hh + 1) * 512],
                                    MGc[:, 1:2], None, ALU.is_gt)
                    V.scalar_tensor_tensor(cdst[:, hh * 512:(hh + 1) * 512],
                                           ARA[:, hh * 512:(hh + 1) * 512],
                                           MGc[:, 4:5], WS2[:, 0:512],
                                           ALU.is_lt, ALU.subtract)
                vtick()
            p5_q_ve[0] = VE.n
            p5_q_ac[0] = AC.n
        TN.wait_ge(VE.s, VE.n)
        for sub in range(4):
            ocol = og * 512 + sub * 128
            if og > 0 or sub > 0:
                TN.wait_ge(VE.s, VE.n)
            for hh in range(16):
                TN.matmul(PSA[:, 0:512], CGt[:, hh * 512 + sub * 128:hh * 512 + sub * 128 + 128],
                          KQ[:, hh * 1024:hh * 1024 + 512], start=(hh == 0), stop=(hh == 15))
                TN.matmul(PSA[:, 512:1024], CGt[:, hh * 512 + sub * 128:hh * 512 + sub * 128 + 128],
                          KQ[:, hh * 1024 + 512:(hh + 1) * 1024], start=(hh == 0), stop=(hh == 15))
                TN.matmul(PSB[:, 0:512], CUt[:, hh * 512 + sub * 128:hh * 512 + sub * 128 + 128],
                          KQ[:, hh * 1024:hh * 1024 + 512], start=(hh == 0), stop=(hh == 15))
                TN.matmul(PSB[:, 512:1024], CUt[:, hh * 512 + sub * 128:hh * 512 + sub * 128 + 128],
                          KQ[:, hh * 1024 + 512:(hh + 1) * 1024], start=(hh == 0), stop=(hh == 15))
            ttick()
            for th in range(2):
                stg = STG[stg_i]
                V.wait_ge(TE.s, TE.n)
                V.wait_ge(STG_S[stg_i].s, STG_S[stg_i].n)
                V.tensor_tensor(EV0[:, :], PSB[:, th * 512:th * 512 + 512],
                                BCD[:, th * 512:th * 512 + 512], ALU.mult)
                V.tensor_tensor(EV1[:, :], PSA[:, th * 512:th * 512 + 512],
                                BCC[:, th * 512:th * 512 + 512], ALU.mult)
                vtick()
                A.wait_ge(VE.s, VE.n)
                A.activation(EV1[:, :], EV1[:, :], AF.Silu)
                atick()
                V.wait_ge(AC.s, AC.n)
                V.tensor_tensor(stg[:, 0:512], EV1[:, :], EV0[:, :], ALU.mult)
                vtick()
                GP.wait_ge(VE.s, VE.n)
                GP.dma_start(interH[ocol:ocol + 128, th * 512:th * 512 + 512],
                             stg[:, 0:512]).then_inc(STG_S[stg_i].s, 16)
                STG_S[stg_i].n += 16
                stg_i = (stg_i + 1) % 3
    barrier()
    if PH <= 9:
        tail(); return nc

    for s in STG_S:
        SY.wait_ge(s.s, s.n)
    FBv = ARC[:, 0:8192]
    SCR = ARC[:, 8192:9216]
    KD = ARB
    p6_cast_ve = [VE.n]
    for th in range(2):
        for tf in range(4):
            t0 = th * 512 + tf * 128
            SY.wait_ge(VE.s, VE.n if os.environ.get("NOPF6A") else p6_cast_ve[0])
            for seg in range(8):
                ld(ARA[:, seg * 1024:(seg + 1) * 1024].rearrange("p (a c) -> p a c", a=8),
                   interH[seg * 1024:(seg + 1) * 1024, t0:t0 + 128]
                   .rearrange("(a p) c -> p a c", p=128))
            V.wait_ge(LD.s, LD.n)
            V.tensor_copy(CGt[:, 0:8192], ARA[:, 0:8192])
            V.memset(FMX[:, :], -1e30)
            V.memset(FMN[:, :], 1e30)
            vtick()
            p6_cast_ve[0] = VE.n
            TN.wait_ge(VE.s, VE.n)
            for i0 in range(0, 64, 4):
                ps = PSC if (i0 // 4) % 2 == 0 else PSD
                if i0 >= 8:
                    TN.wait_ge(VE.s, VE.n)
                for w in range(4):
                    TN.matmul(ps[:, w * 128:(w + 1) * 128],
                              CGt[:, (i0 + w) * 128:(i0 + w + 1) * 128],
                              HADt[:, :], start=True, stop=True)
                ttick()
                V.wait_ge(TE.s, TE.n)
                for w in range(4):
                    V.tensor_copy(FBv[:, i0 + w:i0 + w + 64 * 127 + 1:64],
                                  ps[:, w * 128:(w + 1) * 128])
                vtick()
            TN.wait_ge(VE.s, VE.n)
            for q0 in range(0, 64, 4):
                if q0 > 0:
                    TN.wait_ge(VE.s, VE.n)
                for w in range(4):
                    TN.transpose(PST[:, w * 128:(w + 1) * 128],
                                 FBv[:, (q0 + w) * 128:(q0 + w + 1) * 128], IDT[:, :])
                ttick()
                V.wait_ge(TE.s, TE.n)
                V.tensor_copy(SCR[:, 0:512], PST[:, 0:512])
                vtick()
                TN.wait_ge(VE.s, VE.n)
                for w in range(4):
                    TN.matmul(PSE[:, w * 128:(w + 1) * 128], BDH[:, :],
                              SCR[:, w * 128:(w + 1) * 128], start=True, stop=True)
                ttick()
                V.wait_ge(TE.s, TE.n)
                for w in range(4):
                    V.tensor_tensor(FMX[:, :], FMX[:, :], PSE[:, w * 128:(w + 1) * 128], ALU.max)
                    V.tensor_tensor(FMN[:, :], FMN[:, :], PSE[:, w * 128:(w + 1) * 128], ALU.min)
                V.tensor_copy(FBv[:, q0 * 128:(q0 + 4) * 128], PSE[:, 0:512])
                vtick()
            TN.wait_ge(VE.s, VE.n)
            V.tensor_scalar(FMN[:, :], FMN[:, :], -1.0, None, ALU.mult)
            V.tensor_tensor(FMX[:, :], FMX[:, :], FMN[:, :], ALU.max)
            vtick()
            TN.wait_ge(VE.s, VE.n)
            TN.transpose(PST[:, 0:128], FMX[:, :], IDT[:, :])
            ttick()
            V.wait_ge(TE.s, TE.n)
            V.reduce_max(SC1[:, 0:1], PST[:, 0:128], AX.X)
            V.tensor_scalar(SC1[:, 0:1], SC1[:, 0:1], ALPHA, EPS_FFN, ALU.mult, ALU.max)
            V.reciprocal(SC2[:, 0:1], SC1[:, 0:1])
            V.tensor_scalar(SC2[:, 0:1], SC2[:, 0:1], 128.0 * ALPHA, None, ALU.mult)
            V.tensor_scalar(SC1[:, 1:2], SC1[:, 0:1], MGs[:, 2:3], 1.0 / 256.0, ALU.mult, ALU.mult)
            vtick()
            TN.wait_ge(VE.s, VE.n)
            bcast_col(BCC[:, tf * 128:(tf + 1) * 128], SC2[:, 0:1])
            bcast_col(BCD[:, tf * 128:(tf + 1) * 128], SC1[:, 1:2])
            for q in range(64):
                V.tensor_tensor(EV0[:, 0:128], FBv[:, q * 128:(q + 1) * 128],
                                BCC[:, tf * 128:(tf + 1) * 128], ALU.mult)
                V.tensor_scalar(EV0[:, 0:128], EV0[:, 0:128], MAGIC, MAGIC, ALU.add, ALU.subtract)
                V.tensor_scalar(KD[:, q * 512 + tf * 128:q * 512 + tf * 128 + 128],
                                EV0[:, 0:128], 127.0, None, ALU.min)
            vtick()
        for o2 in range(16):
            SY.wait_ge(VE.s, VE.n if os.environ.get("NOPF6B") else p6_cast_ve[0])
            for seg in range(8):
                ld(ARA[:, seg * 1024:(seg + 1) * 1024].rearrange("p (a c) -> p a c", a=8),
                   WdP2[o2, seg * 1024:(seg + 1) * 1024, :].rearrange("(a p) c -> p a c", p=128))
            A.wait_ge(LD.s, LD.n)
            for gc in range(8):
                A.wait_ge(VE.s, VE.n)
                A.activation(WS2[:, 0:1024], ARA[:, gc * 1024:(gc + 1) * 1024],
                             AF.Sign, bias=BN5[:, :], scale=MGi[:, 2:3])
                A.activation(KQR[:, 0:1024], ARA[:, gc * 1024:(gc + 1) * 1024],
                             AF.Sign, bias=BP5[:, :], scale=MGi[:, 2:3])
                atick()
                V.wait_ge(AC.s, AC.n)
                V.tensor_tensor(CUt[:, gc * 1024:(gc + 1) * 1024],
                                WS2[:, 0:1024], KQR[:, 0:1024], ALU.add)
                vtick()
            p6_cast_ve[0] = VE.n
            SY.wait_ge(VE.s, VE.n)
            ld(WF0[:, 0:512], xT[o2 * 128:(o2 + 1) * 128, th * 512:th * 512 + 512])
            ld(WF1[:, 0:512], sH[o2 * 128:(o2 + 1) * 128, th * 512:th * 512 + 512])
            TN.wait_ge(VE.s, VE.n)
            for g in range(64):
                TN.matmul(PSA[:, 0:512], CUt[:, g * 128:(g + 1) * 128],
                          KD[:, g * 512:g * 512 + 512], start=(g == 0), stop=(g == 63))
            ttick()
            V.wait_ge(TE.s, TE.n)
            V.tensor_tensor(EV0[:, :], PSA[:, 0:512], BCD[:, 0:512], ALU.mult)
            vtick()
            A.wait_ge(VE.s, VE.n)
            A.activation(EV1[:, :], EV0[:, :], AF.Silu)
            atick()
            V.wait_ge(AC.s, AC.n)
            V.wait_ge(LD.s, LD.n)
            V.tensor_tensor(EV2[:, :], EV1[:, :], WF1[:, 0:512], ALU.mult)
            V.tensor_tensor(EV2[:, :], EV2[:, :], WF0[:, 0:512], ALU.add)
            vtick()
            A.wait_ge(VE.s, VE.n)
            A.dma_start(outT[o2 * 128:(o2 + 1) * 128, th * 512:th * 512 + 512],
                        EV2[:, :]).then_inc(STS.s, 16)
            STS.n += 16
            A.wait_ge(STS.s, STS.n)
            TN.wait_ge(VE.s, VE.n)
    barrier()
    A.wait_ge(STS.s, STS.n)
    return nc


_NC_CACHE = {}


def kernel(L0_attn_x, L0_attn_slope_rate, L0_attn_W_qkvu, L0_attn_W_out,
           L0_ffn_w_gate, L0_ffn_w_up, L0_ffn_w_down):
    x = np.asarray(L0_attn_x, np.float32)[0]
    slope = np.asarray(L0_attn_slope_rate, np.float32)[:, 0, 0]
    WT = np.ascontiguousarray(np.asarray(L0_attn_W_qkvu, np.float32).T)
    WoT = np.ascontiguousarray(np.asarray(L0_attn_W_out, np.float32).T)
    WgTh = np.ascontiguousarray(np.asarray(L0_ffn_w_gate, np.float32).T)
    WuTh = np.ascontiguousarray(np.asarray(L0_ffn_w_up, np.float32).T)
    WdT = np.ascontiguousarray(np.asarray(L0_ffn_w_down, np.float32).T)
    WdTp = WdT[_fwht_perm(), :]
    WdP2 = np.ascontiguousarray(WdTp.reshape(8192, 16, 128).transpose(1, 0, 2))
    Wq4 = np.ascontiguousarray(WT.reshape(16, 128, 64, 128).transpose(2, 1, 0, 3)
                               .reshape(64, 128, 2048))
    Wo4 = np.ascontiguousarray(WoT.reshape(16, 128, 16, 128).transpose(2, 1, 0, 3)
                               .reshape(16, 128, 2048))
    H128 = _hadamard(128).astype(BFD)
    H64 = _hadamard(64)
    BD64 = np.zeros((128, 128), np.float32)
    BD64[:64, :64] = H64
    BD64[64:, 64:] = H64
    p = np.arange(128, dtype=np.float32)[:, None]
    consts = dict(
        H128c=H128, BD64c=BD64.astype(BFD),
        IDb=np.eye(128, dtype=np.float32).astype(BFD),
        ONESTc=np.ones((128, 128), np.float32),
        SCONEc=np.full((128, 128), 1.0 / 2048.0, np.float32).astype(BFD),
        ONESBc=np.ones((128, 128), np.float32).astype(BFD),
        ARR1=np.broadcast_to(np.arange(1, 257, dtype=np.float32)[None, :], (128, 256)).copy(),
        IDXD=np.concatenate([
            np.where(np.arange(256)[None, :] - (sh * 128 + p) >= 0,
                     np.arange(256, dtype=np.float32)[None, :] - (sh * 128 + p),
                     1e5).astype(np.float32)
            for sh in range(2)], axis=1),
        ARRC=np.concatenate([1023.0 - (tt * 128 + p) for tt in range(8)], axis=1).astype(np.float32),
        ARRB=np.concatenate([255.0 - (sh * 128 + p) for sh in range(2)], axis=1).astype(np.float32),
        sl_col=np.broadcast_to(slope[None, :], (128, 16)).copy(),
        Wqkvu=WT, Wq4=Wq4, Wo4=Wo4, WgT=WgTh, WuT=WuTh, WdP2=WdP2,
    )
    xall = np.ascontiguousarray(x.T)
    in_maps = []
    for c in range(8):
        xprev = np.zeros((2048, 2048), np.float32)
        if c >= 2:
            xprev[:, 0:1024] = xall[:, (c - 2) * 1024:(c - 1) * 1024]
        if c >= 1:
            xprev[:, 1024:2048] = xall[:, (c - 1) * 1024:c * 1024]
        m = dict(consts)
        m["xT"] = np.ascontiguousarray(xall[:, c * 1024:(c + 1) * 1024])
        m["xprevT"] = xprev
        in_maps.append(m)
    key = os.environ.get("KPHASES", "99")
    if key not in _NC_CACHE:
        _NC_CACHE[key] = build()
    res = run_bass_kernel_spmd(_NC_CACHE[key], in_maps, core_ids=list(range(8)))
    out = np.concatenate([res.results[c]["outT"].T for c in range(8)], axis=0)
    return np.ascontiguousarray(out, dtype=np.float32).reshape(1, 8192, 2048)
